# version 05 (final)
# baseline (speedup 1.0000x reference)
import contextlib
import math

import numpy as np
import ml_dtypes

import concourse.bass as bass
import concourse.mybir as mybir
from concourse.bass_utils import run_bass_kernel_spmd

F32 = mybir.dt.float32
BF16 = mybir.dt.bfloat16
I32 = mybir.dt.int32
AF = mybir.ActivationFunctionType
OP = mybir.AluOpType
AX = mybir.AxisListType

S = 2048
D = 1024
H = 16
FF = 4096
NT = S // 128
ND = D // 128
NF = FF // 128
RK = 1024.0
KB = 1024
QMIN, QMAX = -128.0, 127.0


def _host_bucket_table():
    rel = np.arange(4095, dtype=np.int64) - 2047
    nb = 16
    buckets = np.where(rel > 0, nb, 0)
    ap = np.abs(rel)
    max_exact = nb // 2
    rel_if_large = max_exact + (
        np.log(np.maximum(ap, 1).astype(np.float32) / max_exact)
        / math.log(128 / max_exact) * (nb - max_exact)
    ).astype(np.int64)
    rel_if_large = np.minimum(rel_if_large, nb - 1)
    return (buckets + np.where(ap < max_exact, ap, rel_if_large)).astype(np.int64)


def host_constants():
    cons = {}
    cons["c_ident"] = np.eye(128, dtype=np.float32)
    bt = _host_bucket_table()
    oh = np.zeros((32, 4095), dtype=np.float32)
    oh[bt, np.arange(4095)] = 1.0
    cons["c_onehotT"] = oh
    cons["c_ones"] = np.ones((128, 128), dtype=np.float32)
    blk = np.zeros((128, 128), dtype=np.float32)
    blk[0:64, 0:64] = 1.0
    blk[64:128, 64:128] = 1.0
    cons["c_blk"] = blk.astype(ml_dtypes.bfloat16)
    pat = np.zeros((16, 128), dtype=np.float32)
    pat[:, 0:64] = 1.0
    cons["c_patA"] = pat.copy()
    cons["c_patB"] = np.ascontiguousarray(1.0 - pat)
    return cons


class Sem:
    def __init__(self, nc, ctx, name):
        self.h = ctx.enter_context(nc.semaphore(name))
        self.n = 0

    def inc(self, instr, k=1):
        instr.then_inc(self.h, k)
        self.n += k
        return instr


def build_program(dbg=(), ranks=8):
    nc = bass.Bass("TRN2", target_bir_lowering=False,
                   detect_race_conditions=False)
    dbg = set(dbg)

    x_in = nc.dram_tensor("x", [S, D], F32, kind="ExternalInput").ap()
    c_in = nc.dram_tensor("c", [1, D], F32, kind="ExternalInput").ap()
    wq_in = nc.dram_tensor("wq", [D, D], F32, kind="ExternalInput").ap()
    wk_in = nc.dram_tensor("wk", [D, D], F32, kind="ExternalInput").ap()
    wv_in = nc.dram_tensor("wv", [D, D], F32, kind="ExternalInput").ap()
    wo_in = nc.dram_tensor("wo", [D, D], F32, kind="ExternalInput").ap()
    rel_in = nc.dram_tensor("rel_emb", [32, H], F32, kind="ExternalInput").ap()
    w1_in = nc.dram_tensor("W1", [FF, D], F32, kind="ExternalInput").ap()
    b1_in = nc.dram_tensor("b1", [1, FF], F32, kind="ExternalInput").ap()
    w2_in = nc.dram_tensor("W2", [D, FF], F32, kind="ExternalInput").ap()
    b2_in = nc.dram_tensor("b2", [1, D], F32, kind="ExternalInput").ap()
    wmod_in = nc.dram_tensor("wmod", [D, 6 * D], F32, kind="ExternalInput").ap()
    ident_in = nc.dram_tensor("c_ident", [128, 128], F32, kind="ExternalInput").ap()
    onehot_in = nc.dram_tensor("c_onehotT", [32, 4095], F32, kind="ExternalInput").ap()
    ones_in = nc.dram_tensor("c_ones", [128, 128], F32, kind="ExternalInput").ap()
    blk_in = nc.dram_tensor("c_blk", [128, 128], BF16, kind="ExternalInput").ap()
    patA_in = nc.dram_tensor("c_patA", [16, 128], F32, kind="ExternalInput").ap()
    patB_in = nc.dram_tensor("c_patB", [16, 128], F32, kind="ExternalInput").ap()
    y_out = nc.dram_tensor("y", [S, D], F32, kind="ExternalOutput").ap()

    xT_park = nc.dram_tensor("xT_park", [128, ND * S], F32).ap()
    x2T_park = nc.dram_tensor("x2T_park", [128, ND * S], F32).ap()
    g_spill = nc.dram_tensor("g_spill", [FF, S], F32).ap()
    expd_dram = nc.dram_tensor("expd_dram", [H, 512], F32).ap()
    wq1n_dram = nc.dram_tensor("wq1n_dram", [FF, D], BF16).ap()
    wq2n_dram = nc.dram_tensor("wq2n_dram", [D, FF], BF16).ap()
    cc1_i = nc.dram_tensor("cc1_i", [128, 2], F32)
    cc1_o = nc.dram_tensor("cc1_o", [128, 2], F32)
    cc2_i = nc.dram_tensor("cc2_i", [128, 2], F32)
    cc2_o = nc.dram_tensor("cc2_o", [128, 2], F32)

    dbg_outs = {}

    def tap(name, shape, dtype=F32):
        if name in dbg:
            dbg_outs[name] = nc.dram_tensor(
                "dbg_" + name, shape, dtype, kind="ExternalOutput").ap()
        return dbg_outs.get(name)

    MAPSZ = 207 * KB
    nc.alloc_sbuf_tensor("arena", [128, MAPSZ], mybir.dt.uint8)
    SBASE = nc.sbuf_base - MAPSZ

    def at(name, off, shape, dtype):
        sz = int(np.prod(shape[1:])) * mybir.dt.size(dtype)
        assert off + sz <= MAPSZ, (name, off, sz)
        return nc.alloc_sbuf_tensor_at(name, shape, dtype, offset=SBASE + off)

    _po = [198 * KB]

    def persist(name, shape, dtype):
        t = at(name, _po[0], shape, dtype)
        _po[0] += (int(np.prod(shape[1:])) * mybir.dt.size(dtype) + 31) & ~31
        return t

    ident = persist("ident", [128, 128], F32)
    onesf = persist("onesf", [128, 128], F32)
    blkb = persist("blkb", [128, 128], BF16)
    modT = persist("modT", [128, 64], F32)
    statc = persist("statc", [128, 16], F32)
    sv = persist("sv", [128, 16], F32)
    svb = persist("svb", [128, 16], F32)
    relemb = persist("relemb", [32, H], F32)
    dstat = persist("dstat", [16, 4], F32)
    dpair = persist("dpair", [1, 4], F32)
    epsv = persist("epsv", [128, 1], F32)
    patA = persist("patA", [16, 128], F32)
    patB = persist("patB", [16, 128], F32)
    expdFt = persist("expdFt", [16, 512], F32)
    cbias = persist("cbias", [128, 8], F32)
    b1T = persist("b1T", [128, NF], F32)
    w1sT = persist("w1sT", [128, NF], F32)
    b2T = persist("b2T", [128, ND], F32)
    w2sT = persist("w2sT", [128, ND], F32)
    bb1T = persist("bb1T", [128, NF], F32)
    bb2T = persist("bb2T", [128, ND], F32)
    sm1 = persist("sm1", [1, 16], F32)
    grow = persist("grow", [1, 512], F32)
    fbuf2 = at("fbuf2", 128 * KB, [128, 1024], F32)
    fbuf = persist("fbuf", [128, 128], F32)
    rrec = at("rrec", 194 * KB, [1, 1024], F32)

    fence_ref = []

    def fence(eng, n=2):
        s_fn = fence_ref[0]
        ins = eng.tensor_copy(fbuf[:, 0:64], fbuf[:, 64:128])
        s_fn.inc(ins)
        eng.wait_ge(s_fn.h, s_fn.n)
    assert _po[0] <= 207 * KB, _po[0]

    xT = at("xT", 134 * KB, [128, ND * S], F32)
    h1_hi = at("h1_hi", 0, [128, ND * S], BF16)
    h1_lo = at("h1_lo", 32 * KB, [128, ND * S], BF16)
    aoT_hi = at("aoT_hi", 64 * KB, [128, ND * S], BF16)
    aoT_lo = at("aoT_lo", 96 * KB, [128, ND * S], BF16)

    xnat = at("xnat", 0, [128, NT * D], F32)
    wmods = at("wmods", 64 * KB, [128, 2 * 6 * D], F32)
    crow = at("crow", 112 * KB, [1, D], F32)
    scrow = at("scrow", 116 * KB, [1, D], F32)
    onehot = at("onehot", 64 * KB, [32, 4095], F32)
    biasTt = at("biasTt", 80 * KB, [16, 4095], F32)
    mean_bc = at("mean_bc", 96 * KB, [128, S], F32)
    rstd_bc = at("rstd_bc", 104 * KB, [128, S], F32)
    lnsc = at("lnsc", 112 * KB, [128, S], F32)

    wabs = at("wabs", 120 * KB, [128, 2 * D], F32)

    qTh = at("qTh", 134 * KB, [128, S], BF16)
    qTl = at("qTl", 138 * KB, [128, S], BF16)
    kTh = at("kTh", 142 * KB, [128, S], BF16)
    kTl = at("kTl", 146 * KB, [128, S], BF16)
    ef32a = at("ef32a", 150 * KB, [128, 1024], F32)
    ef32b = at("ef32b", 154 * KB, [128, 1024], F32)
    ehia = at("ehia", 158 * KB, [128, 1024], BF16)
    ehib = at("ehib", 160 * KB, [128, 1024], BF16)
    eloa = at("eloa", 162 * KB, [128, 1024], BF16)
    elob = at("elob", 164 * KB, [128, 1024], BF16)
    sqb = at("sqb", 150 * KB, [128, S], BF16)
    oddst = at("oddst", 158 * KB, [64, 2 * 1024], BF16)
    v_hi = at("v_hi", 166 * KB, [128, NT * 2 * 65], BF16)
    v_lo = at("v_lo", 166 * KB + 4192, [128, NT * 2 * 64], BF16)
    rep0 = at("rep0", 175 * KB, [128, 382], F32)
    rep1 = at("rep1", 176 * KB + 512, [128, 382], F32)
    aev = at("aev", 178 * KB, [128, 1024], F32)
    aev2 = at("aev2", 182 * KB, [128, 1024], F32)
    vT32 = at("vT32", 178 * KB, [128, S], F32)
    wslab = at("wslab", 186 * KB, [128, 1024], F32)
    wslh = at("wslh", 190 * KB, [128, 1024], BF16)
    wsll = at("wsll", 192 * KB, [128, 1024], BF16)
    pvec = at("pvec", 175 * KB - 64, [128, 4], F32)

    wo_hi = at("wo_hi", 0, [128, ND * D], BF16)
    wo_lo = at("wo_lo", 16 * KB, [128, ND * D], BF16)
    wosl = at("wosl", 32 * KB, [128, 2 * D], F32)
    wabsD = at("wabsD", 40 * KB, [128, 2 * D], F32)
    h2T = at("h2T", 64 * KB, [128, ND * S], F32)
    mean2_bc = at("mean2_bc", 0, [128, S], F32)
    rstd2_bc = at("rstd2_bc", 8 * KB, [128, S], F32)
    ln2sc = at("ln2sc", 56 * KB, [128, S], F32)

    eq_u = at("eq_u", 24 * KB, [128, S], F32)
    eq_i32 = at("eq_i32", 32 * KB, [128, S], I32)
    eq_if = at("eq_if", 40 * KB, [128, S], F32)
    eq_d = at("eq_d", 48 * KB, [128, S], F32)
    xqT = at("xqT", 134 * KB, [128, ND * S], BF16)
    wq2T = at("wq2T", 0, [128, NF * D], BF16)
    gbuf = at("gbuf", 64 * KB, [128, S], F32)
    gbuf2 = at("gbuf2", 72 * KB, [128, S], F32)
    fsc = at("fsc", 80 * KB, [128, S], F32)
    fsc2 = at("fsc2", 88 * KB, [128, S], F32)
    fsi = at("fsi", 96 * KB, [128, S], I32)
    w2sl = at("w2sl", 104 * KB, [128, 2 * S], F32)
    wq2nb = at("wq2nb", 112 * KB, [128, 2 * 1024], BF16)
    wq1T_full = at("wq1T_full", 24 * KB, [128, ND * 512], BF16)
    w1sl = at("w1sl", 166 * KB, [128, 4 * D], F32)
    wq1nb = at("wq1nb", 182 * KB, [128, 4 * D], BF16)

    gqT = at("gqT", 64 * KB, [128, NF * 512], BF16)
    grel = at("grel", 96 * KB, [128, 2 * S], F32)
    x2q = at("x2q", 112 * KB, [128, ND * 512], F32)
    ffsc = at("ffsc", 134 * KB, [128, S], F32)
    ffsc2 = at("ffsc2", 142 * KB, [128, S], F32)
    ffsi = at("ffsi", 150 * KB, [128, S], I32)
    ffsd = at("ffsd", 158 * KB, [128, S], F32)
    x3q = at("x3q", 134 * KB, [128, ND * 512], F32)
    ystage = at("ystage", 150 * KB, [128, 2 * D], F32)

    ctx = contextlib.ExitStack()
    with ctx:
        p_a = ctx.enter_context(nc.psum_tensor("p_a", [128, 2048], F32))
        p_b = ctx.enter_context(nc.psum_tensor("p_b", [128, 2048], F32))

        s_ld = Sem(nc, ctx, "s_ld")
        s_sp = Sem(nc, ctx, "s_sp")
        s_mm = Sem(nc, ctx, "s_mm")
        s_v = Sem(nc, ctx, "s_v")
        s_sc = Sem(nc, ctx, "s_sc")
        s_gp = Sem(nc, ctx, "s_gp")
        s_st = Sem(nc, ctx, "s_st")
        s_oh = Sem(nc, ctx, "s_oh")
        s_ws = Sem(nc, ctx, "s_ws")
        s_wa = Sem(nc, ctx, "s_wa")
        s_cp = Sem(nc, ctx, "s_cp")
        s_rp = Sem(nc, ctx, "s_rp")
        s_sh = Sem(nc, ctx, "s_sh")

        cqv = persist("cqv", [128, 1], F32)
        ckv = persist("ckv", [128, 1], F32)
        s_xr = Sem(nc, ctx, "s_xr")
        s_wo0 = Sem(nc, ctx, "s_wo0")
        s_wo1 = Sem(nc, ctx, "s_wo1")
        s_x2s = Sem(nc, ctx, "s_x2s")
        s_cs1 = Sem(nc, ctx, "s_cs1")
        s_cs2 = Sem(nc, ctx, "s_cs2")
        s_gs = Sem(nc, ctx, "s_gs")
        s_w1n0 = Sem(nc, ctx, "s_w1n0")
        s_w1n1 = Sem(nc, ctx, "s_w1n1")
        s_w1s = Sem(nc, ctx, "s_w1s")
        s_w1t = Sem(nc, ctx, "s_w1t")
        s_w2n = Sem(nc, ctx, "s_w2n")
        s_w2s = Sem(nc, ctx, "s_w2s")
        s_w2t = Sem(nc, ctx, "s_w2t")
        s_gr0 = Sem(nc, ctx, "s_gr0")
        s_gr1 = Sem(nc, ctx, "s_gr1")
        s_x2r = Sem(nc, ctx, "s_x2r")
        s_y0 = Sem(nc, ctx, "s_y0")
        s_y1 = Sem(nc, ctx, "s_y1")
        s_fn = Sem(nc, ctx, "s_fn")
        fence_ref.append(s_fn)
        s_xn = Sem(nc, ctx, "s_xn")
        s_cr1 = Sem(nc, ctx, "s_cr1")
        s_cr2 = Sem(nc, ctx, "s_cr2")
        s_wm0 = Sem(nc, ctx, "s_wm0")
        s_wm1 = Sem(nc, ctx, "s_wm1")
        s_pk = Sem(nc, ctx, "s_pk")
        s_ed = Sem(nc, ctx, "s_ed")

        vspl = []
        pvec_mm_prev = [0]
        wospl = []
        x2ev = []
        mm_ln2 = []
        eqsc = []
        eqgpl = []
        eqv = []
        w1sc = []
        w1q = []
        gsc_l = []
        gqsc = []
        gqgp = []
        x3ev = {}
        yev = {}
        sy_tot = [0, 0]
        w1mm = []
        g_mm = []
        g_ev = []
        sev, vev = {}, {}
        vvt = {}
        mm_av = {}
        expdone = {}
        vband = {}
        gpdone = {}
        avev = {}
        band_done = {}
        v_pvec_t = {}

        block = ctx.enter_context(nc.Block())


        @block.sync
        def _(sync):
            for dst, src in [(ident, ident_in), (onesf, ones_in),
                             (blkb, blk_in), (relemb, rel_in),
                             (patA, patA_in), (patB, patB_in),
                             (crow, c_in)]:
                s_ld.inc(sync.dma_start(out=dst[:, :], in_=src), 16)
            s_xn.inc(sync.dma_start(
                out=xnat[:, :],
                in_=bass.AP(x_in.tensor, 0, [[D, 128], [128 * D, NT], [1, D]])), 16)
        LD_CONST = 7 * 16
        LD_X = s_xn.n

        @block.vector
        def _(vector):
            vector.memset(sv[:, :], 0.0)
            vector.memset(fbuf[:, :], 0.0)
            s_v.inc(vector.memset(epsv[:, :], 1e-5))
        V_INIT = s_v.n

        @block.scalar
        def _(scalar):
            scalar.wait_ge(s_ld.h, LD_CONST)
            s_sc.inc(scalar.activation(scrow[:, :], crow[:, :], AF.Silu))
        SC_SILU = s_sc.n

        @block.tensor
        def _(tensor):
            tensor.wait_ge(s_sc.h, SC_SILU)
            for k in range(ND):
                ins = tensor.transpose(p_a[:, 48 + k:49 + k],
                                       scrow[0:1, k * 128:(k + 1) * 128],
                                       ident[0:1, 0:1])
                if k == ND - 1:
                    s_mm.inc(ins)
        MM_SILT = s_mm.n

        @block.vector
        def _(vector):
            vector.wait_ge(s_mm.h, MM_SILT)
            s_v.inc(vector.tensor_copy(modT[:, 48:56], p_a[:, 48:56]))
        V_SILT = s_v.n

        @block.sync
        def _(sync):
            for k in range(ND):
                if k >= 2:
                    sync.wait_ge(s_mm.h, MM_SILT + (k - 1))
                sem = s_wm0 if k % 2 == 0 else s_wm1
                sem.inc(sync.dma_start(
                    out=wmods[:, (k % 2) * 6 * D:(k % 2 + 1) * 6 * D],
                    in_=wmod_in[k * 128:(k + 1) * 128, :]), 16)
            sync.wait_ge(s_mm.h, MM_SILT + 7)
            s_oh.inc(sync.dma_start(out=onehot[:, :], in_=onehot_in), 16)
        LD_ONEHOT = s_oh.n

        @block.tensor
        def _(tensor):
            tensor.wait_ge(s_v.h, V_SILT)
            for k in range(ND):
                sem = s_wm0 if k % 2 == 0 else s_wm1
                tensor.wait_ge(sem.h, 16 * (k // 2 + 1))
                base = (k % 2) * 6 * D
                for m in range(48):
                    ins = tensor.matmul(
                        p_b[:, m:m + 1],
                        wmods[:, base + m * 128:base + (m + 1) * 128],
                        modT[:, 48 + k:49 + k],
                        start=(k == 0 and m == 0), stop=(k == ND - 1 and m == 47),
                        skip_group_check=True)
                    if m == 47:
                        s_mm.inc(ins)
        MM_MOD = s_mm.n

        @block.vector
        def _(vector):
            vector.wait_ge(s_mm.h, MM_MOD)
            vector.tensor_copy(modT[:, 0:48], p_b[:, 0:48])
            fence(vector)
            for c0 in (16, 40):
                vector.tensor_scalar(modT[:, 48:56], modT[:, c0:c0 + 8],
                                     -1.0, None, OP.mult)
                fence(vector)
                vector.tensor_tensor(modT[:, 48:56], modT[:, 48:56],
                                     modT[:, c0:c0 + 8], OP.max)
                fence(vector)
                vector.tensor_scalar(modT[:, 48:56], modT[:, 48:56],
                                     0.25, None, OP.max)
                vector.tensor_scalar(modT[:, 56:64], modT[:, c0:c0 + 8],
                                     0.0, 2.0, OP.is_ge, OP.mult)
                fence(vector)
                vector.tensor_scalar(modT[:, 56:64], modT[:, 56:64],
                                     1.0, None, OP.subtract)
                fence(vector)
                vector.tensor_tensor(modT[:, c0:c0 + 8], modT[:, 48:56],
                                     modT[:, 56:64], OP.mult)
                fence(vector)
            vector.tensor_scalar(modT[:, 8:16], modT[:, 8:16], 1.0, None, OP.add)
            s_v.inc(vector.tensor_scalar(modT[:, 32:40], modT[:, 32:40],
                                         1.0, None, OP.add))
        V_MOD = s_v.n

        if tap("modT", [128, 64]) is not None:
            @block.sync
            def _(sync):
                sync.wait_ge(s_v.h, V_MOD)
                s_st.inc(sync.dma_start(out=dbg_outs["modT"], in_=modT[:, :]), 16)

        @block.tensor
        def _(tensor):
            tensor.wait_ge(s_xn.h, LD_X)
            for t in range(NT):
                base = (t % 2) * 1024
                if t >= 2:
                    tensor.wait_ge(s_v.h, V_MOD + (t - 1))
                for d in range(ND):
                    ins = tensor.transpose(
                        p_a[:, base + d * 128:base + (d + 1) * 128],
                        xnat[:, t * D + d * 128:t * D + (d + 1) * 128],
                        ident[:, :])
                    if d == ND - 1:
                        s_mm.inc(ins)

        @block.vector
        def _(vector):
            for t in range(NT):
                base = (t % 2) * 1024
                vector.wait_ge(s_mm.h, MM_MOD + t + 1)
                s_v.inc(vector.tensor_copy(
                    bass.AP(xT, t * 128, [[ND * S, 128], [S, ND], [1, 128]]),
                    p_a[:, base:base + 1024]))
        V_XT = s_v.n

        @block.sync
        def _(sync):
            sync.wait_ge(s_v.h, V_XT)
            s_pk.inc(sync.dma_start(out=xT_park, in_=xT[:, :]), 16)

        if tap("xT", [128, ND * S]):
            @block.sync
            def _(sync):
                sync.wait_ge(s_v.h, V_XT)
                s_st.inc(sync.dma_start(out=dbg_outs["xT"], in_=xT[:, :]), 16)

        MM_BIAS0 = s_mm.n

        @block.tensor
        def _(tensor):
            tensor.wait_ge(s_oh.h, LD_ONEHOT)
            tensor.wait_ge(s_v.h, V_XT)
            for c in range(8):
                if c == 4:
                    tensor.wait_ge(s_v.h, V_XT + 1)
                n = 512 if c < 7 else 4095 - 7 * 512
                ins = tensor.matmul(p_a[0:16, c % 4 * 512:c % 4 * 512 + n],
                                    relemb[:, :], onehot[:, c * 512:c * 512 + n],
                                    start=True, stop=True)
                if c % 4 == 3 or c == 7:
                    s_mm.inc(ins)
        MM_BIAS = s_mm.n

        @block.vector
        def _(vector):
            for g in range(2):
                vector.wait_ge(s_mm.h, MM_BIAS0 + 1 + g)
                n = 2048 if g == 0 else 4095 - 2048
                s_v.inc(vector.tensor_copy(biasTt[0:16, g * 2048:g * 2048 + n],
                                           p_a[0:16, 0:n]))
            vector.tensor_tensor(dstat[0:16, 2:3], biasTt[0:16, 0:1],
                                 sv[0:16, 0:1], OP.add)
            fence(vector)
            vector.tensor_scalar(biasTt[0:16, :], biasTt[0:16, :],
                                 dstat[0:16, 2:3], None, OP.subtract)
            vector.tensor_reduce(dstat[0:16, 0:1], biasTt[0:16, :], AX.X, OP.max)
            fence(vector)
            s_v.inc(vector.tensor_tensor(dstat[0:16, 1:2], biasTt[0:16, 4094:4095],
                                         sv[0:16, 0:1], OP.add))
        V_BIAS = s_v.n

        @block.scalar
        def _(scalar):
            scalar.wait_ge(s_v.h, V_BIAS)
            s_sc.inc(scalar.activation(expdFt[0:16, 0:509],
                                       biasTt[0:16, 1793:2302], AF.Exp))
        SC_EXPD = s_sc.n

        @block.sync
        def _(sync):
            sync.wait_ge(s_sc.h, SC_EXPD)
            s_ed.inc(sync.dma_start(out=expd_dram[0:16, 0:509],
                                    in_=expdFt[0:16, 0:509]), 16)
        LD_EXPD = s_ed.n

        @block.scalar
        def _(scalar):
            for d in range(ND):
                scalar.wait_ge(s_v.h, V_XT)
                if d >= 1:
                    scalar.wait_ge(s_mm.h, MM_BIAS + d)
                s_sc.inc(scalar.activation(lnsc[:, :], xT[:, d * S:(d + 1) * S],
                                           AF.Square))
        SC_SQ0 = SC_EXPD

        @block.tensor
        def _(tensor):
            tensor.wait_ge(s_v.h, V_BIAS)
            tensor.wait_ge(s_v.h, V_MOD)
            for d in range(ND):
                for c in range(4):
                    tensor.matmul(p_a[0:1, c * 512:(c + 1) * 512], onesf[:, 0:1],
                                  xT[:, d * S + c * 512:d * S + (c + 1) * 512],
                                  start=(d == 0), stop=(d == ND - 1),
                                  skip_group_check=True)
                tensor.wait_ge(s_sc.h, SC_EXPD + d + 1)
                for c in range(4):
                    ins = tensor.matmul(p_b[0:1, c * 512:(c + 1) * 512],
                                        onesf[:, 0:1],
                                        lnsc[:, c * 512:(c + 1) * 512],
                                        start=(d == 0), stop=(d == ND - 1),
                                        skip_group_check=True)
                    if c == 3:
                        s_mm.inc(ins)
        MM_LN1 = s_mm.n

        @block.vector
        def _(vector):
            vector.wait_ge(s_mm.h, MM_LN1)
            vector.tensor_scalar(mean_bc[0:1, :], p_a[0:1, :], 1.0 / D, None,
                                 OP.mult)
            vector.tensor_scalar(rstd_bc[0:1, :], p_b[0:1, :], 1.0 / D, None,
                                 OP.mult)
            vector.tensor_tensor(lnsc[0:1, :], mean_bc[0:1, :], mean_bc[0:1, :],
                                 OP.mult)
            s_v.inc(vector.tensor_tensor(rstd_bc[0:1, :], rstd_bc[0:1, :],
                                         lnsc[0:1, :], OP.subtract))
        V_LN1V = s_v.n

        @block.scalar
        def _(scalar):
            scalar.wait_ge(s_v.h, V_LN1V)
            scalar.activation(lnsc[0:1, :], rstd_bc[0:1, :], AF.Ln,
                              bias=epsv[0:1, :])
            s_sc.inc(scalar.activation(rstd_bc[0:1, :], lnsc[0:1, :], AF.Exp,
                                       scale=-0.5))
        SC_LN1R = s_sc.n

        @block.tensor
        def _(tensor):
            tensor.wait_ge(s_sc.h, SC_LN1R)
            for c in range(4):
                ins = tensor.matmul(p_a[:, c * 512:(c + 1) * 512], onesf[0:1, :],
                                    mean_bc[0:1, c * 512:(c + 1) * 512],
                                    start=True, stop=True)
                if c == 3:
                    s_mm.inc(ins)
            for c in range(4):
                ins = tensor.matmul(p_b[:, c * 512:(c + 1) * 512], onesf[0:1, :],
                                    rstd_bc[0:1, c * 512:(c + 1) * 512],
                                    start=True, stop=True)
                if c == 3:
                    s_mm.inc(ins)
        MM_LN1B = s_mm.n

        @block.vector
        def _(vector):
            vector.wait_ge(s_mm.h, MM_LN1B - 1)
            vector.tensor_copy(mean_bc[:, :], p_a[:, :])
            vector.wait_ge(s_mm.h, MM_LN1B)
            s_v.inc(vector.tensor_copy(rstd_bc[:, :], p_b[:, :]))
        V_LN1B = s_v.n

        V_H1B = s_v.n

        @block.vector
        def _(vector):
            vector.wait_ge(s_v.h, V_LN1B)
            for d in range(ND):
                vector.tensor_sub(lnsc[:, :], xT[:, d * S:(d + 1) * S],
                                  mean_bc[:, :])
                vector.tensor_mul(lnsc[:, :], lnsc[:, :], rstd_bc[:, :])
                s_v.inc(vector.tensor_scalar(
                    lnsc[:, :], lnsc[:, :],
                    modT[:, 8 + d:9 + d], modT[:, d:d + 1],
                    OP.mult, OP.add))
                vector.wait_ge(s_sc.h, SC_LN1R + d + 1)
                s_v.inc(vector.tensor_sub(h1_lo[:, d * S:(d + 1) * S],
                                          lnsc[:, :],
                                          h1_hi[:, d * S:(d + 1) * S]))
        V_H1 = s_v.n

        @block.scalar
        def _(scalar):
            for d in range(ND):
                scalar.wait_ge(s_v.h, V_H1B + 2 * d + 1)
                s_sc.inc(scalar.copy(h1_hi[:, d * S:(d + 1) * S], lnsc[:, :]))
        SC_H1 = s_sc.n

        if tap("h1_hi", [128, ND * S], BF16) is not None:
            @block.sync
            def _(sync):
                sync.wait_ge(s_v.h, V_H1)
                s_st.inc(sync.dma_start(out=dbg_outs["h1_hi"], in_=h1_hi[:, :]), 16)
        if tap("h1_lo", [128, ND * S], BF16) is not None:
            @block.sync
            def _(sync):
                sync.wait_ge(s_v.h, V_H1)
                s_st.inc(sync.dma_start(out=dbg_outs["h1_lo"], in_=h1_lo[:, :]), 16)
        if tap("mean_bc", [128, S]) is not None:
            @block.sync
            def _(sync):
                sync.wait_ge(s_v.h, V_LN1B)
                s_st.inc(sync.dma_start(out=dbg_outs["mean_bc"], in_=mean_bc[:, :]), 16)
        if tap("rstd_bc", [128, S]) is not None:
            @block.sync
            def _(sync):
                sync.wait_ge(s_v.h, V_LN1B)
                s_st.inc(sync.dma_start(out=dbg_outs["rstd_bc"], in_=rstd_bc[:, :]), 16)

        b1row = at("b1row", 64 * KB, [1, FF], F32)
        b2row = at("b2row", 80 * KB, [1, D], F32)

        @block.sync
        def _(sync):
            sync.wait_ge(s_v.h, V_BIAS)
            s_oh.inc(sync.dma_start(out=b1row[:, :], in_=b1_in), 16)
            s_oh.inc(sync.dma_start(out=b2row[:, :], in_=b2_in), 16)
        LD_B12 = s_oh.n

        vb0 = s_v.n

        @block.tensor
        def _(tensor):
            tensor.wait_ge(s_oh.h, LD_B12)
            tensor.wait_ge(s_v.h, V_LN1B)
            for g in range(8):
                if g >= 1:
                    tensor.wait_ge(s_v.h, vb0 + g)
                for j in range(4):
                    i = 4 * g + j
                    ins = tensor.transpose(p_a[:, 1024 + j:1024 + j + 1],
                                           b1row[0:1, i * 128:(i + 1) * 128],
                                           ident[0:1, 0:1])
                    if j == 3:
                        s_mm.inc(ins)
            tensor.wait_ge(s_v.h, vb0 + 8)
            for j in range(8):
                ins = tensor.transpose(p_a[:, 1032 + j:1032 + j + 1],
                                       b2row[0:1, j * 128:(j + 1) * 128],
                                       ident[0:1, 0:1])
                if j == 7:
                    s_mm.inc(ins)
        MM_BT0 = s_mm.n - 9

        @block.vector
        def _(vector):
            for g in range(8):
                vector.wait_ge(s_mm.h, MM_BT0 + g + 1)
                s_v.inc(vector.tensor_tensor(b1T[:, 4 * g:4 * g + 4],
                                             p_a[:, 1024:1028],
                                             sv[:, 0:4], OP.add))
            vector.wait_ge(s_mm.h, MM_BT0 + 9)
            s_v.inc(vector.tensor_tensor(b2T[:, 0:8], p_a[:, 1032:1040],
                                         sv[:, 0:8], OP.add))
        V_BT = s_v.n

        wmax1 = persist("wmax1", [128, 1], F32)
        wmax2 = persist("wmax2", [128, 1], F32)
        wred = persist("wred", [128, 1], F32)

        @block.vector
        def _(vector):
            s_v.inc(vector.memset(wmax1[:, :], 0.0))
            s_v.inc(vector.memset(wmax2[:, :], 0.0))
        V_WM0 = s_v.n

        @block.sync
        def _(sync):
            sync.wait_ge(s_v.h, V_WM0)
            for i in range(16):
                if i > 0:
                    sync.wait_ge(s_v.h, V_WM0 + i)
                s_wa.inc(sync.dma_start(
                    out=wabs[:, :],
                    in_=bass.AP(w1_in.tensor, i * 256 * D,
                                [[D, 128], [128 * D, 2], [1, D]])), 16)
        LD_WA1 = s_wa.n

        @block.vector
        def _(vector):
            for i in range(16):
                vector.wait_ge(s_wa.h, 16 * (i + 1))
                vector.tensor_reduce(wred[:, :], wabs[:, :], AX.X, OP.max,
                                     apply_absolute_value=True)
                fence(vector)
                s_v.inc(vector.tensor_tensor(wmax1[:, :], wmax1[:, :],
                                             wred[:, :], OP.max))
        V_WA1 = s_v.n

        def chunks_for(kt, q0):
            lend = min(1024, max(0, 128 * kt - 127 - q0))
            bend = min(1024, max(0, 128 * kt + 255 - q0))
            out = []
            if lend > 0:
                out.append((0, lend, 1))
            if bend > lend:
                out.append((lend, bend, 0))
            if bend < 1024:
                out.append((bend, 1024, 0))
            return out, (lend, bend)

        PASSES = [(0, 0), (1, 0), (0, 1)]

        mm_qk = {}
        sh_cnt = [0]

        @block.vector
        def _(vector):
            vector.wait_ge(s_pk.h, 16)
            s_v.inc(vector.memset(v_hi[:, :], 1.0))
        V_VONES = s_v.n

        for p in range(8):
            h0, h1 = 2 * p, 2 * p + 1

            for wi, (wname, w_in) in enumerate([("q", wq_in), ("k", wk_in),
                                                ("v", wv_in)]):
                sidx = 3 * p + wi

                @block.sync
                def _(sync, sidx=sidx, w_in=w_in, p=p, wi=wi):
                    if sidx == 0:
                        sync.wait_ge(s_pk.h, 16)
                    else:
                        sync.wait_ge(s_v.h, vspl[sidx - 1])
                    s_ws.inc(sync.dma_start(
                        out=wslab[:, :],
                        in_=bass.AP(w_in.tensor, p * 128,
                                    [[D, 128], [128 * D, 8], [1, 128]])), 16)

                @block.scalar
                def _(scalar, sidx=sidx):
                    scalar.wait_ge(s_ws.h, 16 * (sidx + 1))
                    if sidx >= 1:
                        scalar.wait_ge(s_mm.h, mm_qk[("proj", sidx - 1)])
                    s_sc.inc(scalar.copy(wslh[:, :], wslab[:, :]))
                sspl_h = s_sc.n

                @block.vector
                def _(vector, sh=sspl_h):
                    vector.wait_ge(s_sc.h, sh)
                    s_v.inc(vector.tensor_sub(wsll[:, :], wslab[:, :],
                                              wslh[:, :]))
                vspl.append(s_v.n)

                buf = p_a if wi in (0, 2) else p_b

                @block.tensor
                def _(tensor, sh=sspl_h, vs=s_v.n, wi=wi, buf=buf, p=p):
                    tensor.wait_ge(s_sc.h, sh)
                    tensor.wait_ge(s_v.h, vs)
                    if wi == 2:
                        tensor.wait_ge(s_v.h, vev[("q", p)])
                        tensor.wait_ge(s_sc.h, sev[("q", p)])
                    ntot = 3 * 8 * 4
                    n = 0
                    for pi, (wl, hl) in enumerate(PASSES):
                        wsrc = wsll if wl else wslh
                        hsrc = h1_lo if hl else h1_hi
                        for kk in range(ND):
                            for c in range(4):
                                n += 1
                                ins = tensor.matmul(
                                    buf[:, c * 512:(c + 1) * 512],
                                    wsrc[:, kk * 128:(kk + 1) * 128],
                                    hsrc[:, kk * S + c * 512:kk * S + (c + 1) * 512],
                                    start=(pi == 0 and kk == 0),
                                    stop=(pi == 2 and kk == ND - 1))
                                if n == ntot:
                                    s_mm.inc(ins)
                mm_qk[("proj", sidx)] = s_mm.n

                if wi < 2:
                    dsth, dstl = (qTh, qTl) if wi == 0 else (kTh, kTl)

                    @block.scalar
                    def _(scalar, mmn=s_mm.n, dsth=dsth, buf=buf):
                        scalar.wait_ge(s_mm.h, mmn)
                        s_sc.inc(scalar.copy(dsth[:, :], buf[:, :]))
                    sev[(wname, p)] = s_sc.n

                    @block.vector
                    def _(vector, scn=s_sc.n, dsth=dsth, dstl=dstl, buf=buf):
                        vector.wait_ge(s_sc.h, scn)
                        s_v.inc(vector.tensor_sub(dstl[:, :], buf[:, :],
                                                  dsth[:, :]))
                    vev[(wname, p)] = s_v.n
                else:
                    @block.vector
                    def _(vector, mmn=s_mm.n):
                        vector.wait_ge(s_mm.h, mmn)
                        s_v.inc(vector.tensor_copy(vT32[:, :], p_a[:, :]))
                    vev[("v", p)] = s_v.n

            for g in range(4):
                @block.tensor
                def _(tensor, p=p, g=g):
                    if g == 0:
                        tensor.wait_ge(s_v.h, vev[("v", p)])
                        tensor.wait_ge(s_v.h, vev[("k", p)])
                    if g >= 2:
                        tensor.wait_ge(s_v.h, vvt[(p, g - 2)])
                    for j in range(4):
                        tt = 4 * g + j
                        ins = tensor.transpose(
                            p_b[:, (g % 2) * 512 + j * 128:
                                   (g % 2) * 512 + (j + 1) * 128],
                            vT32[:, tt * 128:(tt + 1) * 128], ident[:, :])
                        if j == 3:
                            s_mm.inc(ins)
                mm_qk[("vtr", p, g)] = s_mm.n

                @block.vector
                def _(vector, p=p, g=g, mmn=s_mm.n):
                    vector.wait_ge(s_mm.h, mmn)
                    base = (g % 2) * 512
                    vector.tensor_copy(
                        bass.AP(v_hi, (4 * g) * 130,
                                [[NT * 2 * 65, 128], [130, 4], [65, 2], [1, 64]]),
                        bass.AP(p_b, base, [[2048, 128], [128, 4], [64, 2], [1, 64]]))
                    s_v.inc(vector.tensor_tensor(
                        bass.AP(v_lo, (4 * g) * 128,
                                [[NT * 2 * 64, 128], [128, 4], [64, 2], [1, 64]]),
                        bass.AP(p_b, base, [[2048, 128], [128, 4], [64, 2], [1, 64]]),
                        bass.AP(v_hi, (4 * g) * 130,
                                [[NT * 2 * 65, 128], [130, 4], [65, 2], [1, 64]]),
                        OP.subtract))
                vvt[(p, g)] = s_v.n

            @block.vector
            def _(vector, p=p):
                vector.wait_ge(s_sc.h, sev[("q", p)])
                s_v.inc(vector.tensor_tensor(sqb[:, :], qTh[:, :], qTh[:, :],
                                             OP.mult))
            v_sq_q = s_v.n

            @block.tensor
            def _(tensor, p=p, vs=s_v.n):
                tensor.wait_ge(s_v.h, vev[("v", p)])
                tensor.wait_ge(s_v.h, vs)
                for c in range(4):
                    ins = tensor.matmul(p_a[:, c * 512:(c + 1) * 512],
                                        blkb[:, :], sqb[:, c * 512:(c + 1) * 512],
                                        start=True, stop=True)
                    if c == 3:
                        s_mm.inc(ins)
            mm_sq_q = s_mm.n

            @block.vector
            def _(vector, p=p, mmn=s_mm.n):
                vector.wait_ge(s_mm.h, mmn)
                vector.tensor_reduce(cqv[:, :], p_a[:, :], AX.X, OP.max)
                vector.wait_ge(s_sc.h, sev[("k", p)])
                s_v.inc(vector.tensor_tensor(sqb[:, :], kTh[:, :], kTh[:, :],
                                             OP.mult))
            v_sq_k = s_v.n

            @block.tensor
            def _(tensor, vs=s_v.n):
                tensor.wait_ge(s_v.h, vs)
                for c in range(4):
                    ins = tensor.matmul(p_a[:, c * 512:(c + 1) * 512],
                                        blkb[:, :], sqb[:, c * 512:(c + 1) * 512],
                                        start=True, stop=True)
                    if c == 3:
                        s_mm.inc(ins)
            mm_sq_k = s_mm.n

            @block.vector
            def _(vector, mmn=s_mm.n):
                vector.wait_ge(s_mm.h, mmn)
                s_v.inc(vector.tensor_reduce(ckv[:, :], p_a[:, :], AX.X, OP.max))
            v_ck = s_v.n

            @block.sync
            def _(sync, p=p, vs=s_v.n, prevmm=pvec_mm_prev[0]):
                sync.wait_ge(s_v.h, vs)
                if prevmm:
                    sync.wait_ge(s_mm.h, prevmm)
                s_cp.inc(sync.dma_start(out=sm1[0:1, 0:2],
                                        in_=bass.AP(cqv, 0, [[64, 2], [1, 1]])), 16)
                s_cp.inc(sync.dma_start(out=sm1[0:1, 2:4],
                                        in_=bass.AP(ckv, 0, [[64, 2], [1, 1]])), 16)
                s_cp.inc(sync.dma_start(out=sm1[0:1, 4:6],
                                        in_=bass.AP(dstat, 4 * (2 * p),
                                                    [[4, 2], [1, 1]])), 16)
                s_cp.inc(sync.dma_start(out=sm1[0:1, 6:8],
                                        in_=bass.AP(dstat, 4 * (2 * p) + 1,
                                                    [[4, 2], [1, 1]])), 16)

            @block.vector
            def _(vector, cp=s_cp.n):
                vector.wait_ge(s_cp.h, cp)
                s_v.inc(vector.tensor_tensor(sm1[0:1, 8:10], sm1[0:1, 0:2],
                                             sm1[0:1, 2:4], OP.mult))
            v_c2 = s_v.n

            @block.scalar
            def _(scalar, vs=s_v.n):
                scalar.wait_ge(s_v.h, vs)
                s_sc.inc(scalar.activation(sm1[0:1, 10:12], sm1[0:1, 8:10],
                                           AF.Sqrt, scale=1.0404))
            sc_sqrt = s_sc.n

            @block.vector
            def _(vector, scn=s_sc.n):
                vector.wait_ge(s_sc.h, scn)
                vector.tensor_tensor(sm1[0:1, 12:14], sm1[0:1, 10:12],
                                     sm1[0:1, 4:6], OP.add)
                fence(vector)
                vector.scalar_tensor_tensor(sm1[0:1, 12:14], sm1[0:1, 12:14],
                                            0.1, sv[0:1, 0:2], OP.add,
                                            OP.subtract)
                fence(vector)
                vector.tensor_scalar(sm1[0:1, 12:14], sm1[0:1, 12:14], -1.0,
                                     None, OP.mult)
                fence(vector)
                s_v.inc(vector.tensor_tensor(sm1[0:1, 14:16], sm1[0:1, 12:14],
                                             sm1[0:1, 6:8], OP.add))
            v_bias_row = s_v.n

            @block.tensor
            def _(tensor, vs=s_v.n):
                tensor.wait_ge(s_v.h, vs)
                for j, col in enumerate([12, 13, 14, 15]):
                    ins = tensor.matmul(p_a[:, 1024 + j:1025 + j],
                                        onesf[0:1, :], sm1[0:1, col:col + 1],
                                        start=True, stop=True,
                                        skip_group_check=True)
                    if j == 3:
                        s_mm.inc(ins)
            mm_pvec = s_mm.n
            pvec_mm_prev[0] = mm_pvec

            @block.vector
            def _(vector, mmn=s_mm.n):
                vector.wait_ge(s_mm.h, mmn)
                s_v.inc(vector.tensor_copy(pvec[:, 0:4], p_a[:, 1024:1028]))
            v_pvec = s_v.n

            @block.sync
            def _(sync, p=p, vs=s_v.n):
                sync.wait_ge(s_ed.h, LD_EXPD)
                if p > 0:
                    sync.wait_ge(s_v.h, band_done[p - 1])
                for hh, rep in ((h0, rep0), (h1, rep1)):
                    s_rp.inc(sync.dma_start(
                        out=rep[:, :],
                        in_=bass.AP(expd_dram.tensor, 512 * hh,
                                    [[1, 128], [1, 382]])), 16)

            for qh in range(2):
                q0 = 1024 * qh
                for kt in range(NT):
                    ch, (lend, bend) = chunks_for(kt, q0)

                    @block.tensor
                    def _(tensor, p=p, qh=qh, kt=kt, vp=v_pvec):
                        if kt == 0:
                            if qh == 0:
                                tensor.wait_ge(s_v.h, vvt[(p, 3)])
                                tensor.wait_ge(s_v.h, vp)
                            else:
                                tensor.wait_ge(s_v.h, avev[(p, 0)])
                        else:
                            tensor.wait_ge(s_sc.h, expdone[(p, qh, kt - 1)])
                        n = 0
                        for pi, (wl, hl) in enumerate(PASSES):
                            ksrc = kTl if wl else kTh
                            qsrc = qTl if hl else qTh
                            for hh in range(2):
                                pb = 0 if hh == 0 else 1024
                                for c in range(2):
                                    n += 1
                                    ins = tensor.matmul(
                                        p_a[:, pb + c * 512:pb + (c + 1) * 512],
                                        ksrc[hh * 64:hh * 64 + 64,
                                             kt * 128:(kt + 1) * 128],
                                        qsrc[hh * 64:hh * 64 + 64,
                                             q0 + c * 512:q0 + (c + 1) * 512],
                                        start=(pi == 0), stop=(pi == 2))
                                    if n == 12:
                                        s_mm.inc(ins)
                    mm_qk[(p, qh, kt)] = s_mm.n

                    @block.scalar
                    def _(scalar, p=p, qh=qh, kt=kt, ch=ch):
                        scalar.wait_ge(s_mm.h, mm_qk[(p, qh, kt)])
                        last = None
                        for hh, (ebuf, pb) in enumerate([(ef32a, 0),
                                                         (ef32b, 1024)]):
                            for (c0, c1, bc) in ch:
                                last = scalar.activation(
                                    ebuf[:, c0:c1], p_a[:, pb + c0:pb + c1],
                                    AF.Exp, bias=pvec[:, bc * 2 + hh:bc * 2 + hh + 1])
                        s_sc.inc(last)
                    expdone[(p, qh, kt)] = s_sc.n

                    @block.vector
                    def _(vector, p=p, qh=qh, kt=kt, lend=lend, bend=bend,
                          q0=q0, shc=sh_cnt[0]):
                        vector.wait_ge(s_sc.h, expdone[(p, qh, kt)])
                        if kt == 0 and shc:
                            vector.wait_ge(s_sh.h, 16 * shc)
                        if bend > lend:
                            for ebuf, rep in ((ef32a, rep0), (ef32b, rep1)):
                                off = 254 + 128 * kt - q0 - lend
                                vector.tensor_tensor(
                                    ebuf[:, lend:bend], ebuf[:, lend:bend],
                                    bass.AP(rep, off, [[382, 128],
                                                       [-1, bend - lend]]),
                                    OP.mult)
                            fence(vector, 2)
                        vector.tensor_copy(ehia[:, :], ef32a[:, :])
                        vector.tensor_copy(ehib[:, :], ef32b[:, :])
                        s_v.inc(vector.tensor_sub(eloa[:, :], ef32a[:, :],
                                                  ehia[:, :]))
                    vband[(p, qh, kt)] = s_v.n

                    @block.gpsimd
                    def _(gpsimd, p=p, qh=qh, kt=kt):
                        gpsimd.wait_ge(s_v.h, vband[(p, qh, kt)])
                        s_gp.inc(gpsimd.tensor_tensor(elob[:, :], ef32b[:, :],
                                                      ehib[:, :], OP.subtract))
                    gpdone[(p, qh, kt)] = s_gp.n

                    if (p == 0 and qh == 0 and kt == 0
                            and tap("efE", [128, 1024]) is not None):
                        @block.sync
                        def _(sync, vb=vband[(0, 0, 0)]):
                            sync.wait_ge(s_v.h, vb)
                            s_st.inc(sync.dma_start(out=dbg_outs["efE"],
                                                    in_=ef32a[:, :]), 16)

                    @block.tensor
                    def _(tensor, p=p, qh=qh, kt=kt):
                        tensor.wait_ge(s_v.h, vband[(p, qh, kt)])
                        tensor.wait_ge(s_gp.h, gpdone[(p, qh, kt)])
                        n = 0
                        for hh, (ehi, elo, pb) in enumerate(
                                [(ehia, eloa, 0), (ehib, elob, 1024)]):
                            vh0 = kt * 130 + hh * 65
                            vl0 = kt * 128 + hh * 64
                            for pi, (stat, rhs) in enumerate(
                                    [("hi", ehi), ("lo", ehi), ("hi", elo)]):
                                for c in range(2):
                                    n += 1
                                    if stat == "hi":
                                        lh = bass.AP(v_hi, vh0,
                                                     [[NT * 2 * 65, 128], [1, 65]])
                                    else:
                                        lh = bass.AP(v_lo, vl0,
                                                     [[NT * 2 * 64, 128], [1, 64]])
                                    ins = tensor.matmul(
                                        p_b[0:65 if stat == "hi" else 64,
                                            pb + c * 512:pb + (c + 1) * 512],
                                        lh, rhs[:, c * 512:(c + 1) * 512],
                                        start=(kt == 0 and pi == 0),
                                        stop=(kt == NT - 1 and pi == 2))
                                    if n == 12:
                                        s_mm.inc(ins)
                    mm_av[(p, qh, kt)] = s_mm.n

                @block.vector
                def _(vector, p=p, qh=qh):
                    vector.wait_ge(s_mm.h, mm_av[(p, qh, NT - 1)])
                    s_v.inc(vector.reciprocal(rrec[0:1, :], p_b[64:65, 0:1024]))
                v_rc0 = s_v.n

                @block.tensor
                def _(tensor, vs=s_v.n):
                    tensor.wait_ge(s_v.h, vs)
                    for c in range(2):
                        ins = tensor.matmul(
                            p_a[0:64, c * 512:(c + 1) * 512], onesf[0:1, 0:64],
                            rrec[0:1, c * 512:(c + 1) * 512],
                            start=True, stop=True)
                        if c == 1:
                            s_mm.inc(ins)
                mm_bc0 = s_mm.n

                @block.scalar
                def _(scalar, p=p, qh=qh):
                    scalar.wait_ge(s_mm.h, mm_av[(p, qh, NT - 1)])
                    scalar.copy(aev[0:64, :], p_b[0:64, 0:1024])
                    s_sc.inc(scalar.copy(aev2[0:64, :], p_b[0:64, 1024:2048]))
                sc_aev = s_sc.n

                @block.vector
                def _(vector, p=p, qh=qh, mmn=s_mm.n, scn=s_sc.n):
                    vector.wait_ge(s_mm.h, mmn)
                    vector.wait_ge(s_sc.h, scn)
                    dst = p * S + qh * 1024
                    vector.tensor_tensor(aev[0:64, :], aev[0:64, :],
                                         p_a[0:64, 0:1024], OP.mult)
                    vector.tensor_copy(aoT_hi[0:64, dst:dst + 1024],
                                       aev[0:64, :])
                    s_v.inc(vector.tensor_sub(aoT_lo[0:64, dst:dst + 1024],
                                              aev[0:64, :],
                                              aoT_hi[0:64, dst:dst + 1024]))
                v_ev0 = s_v.n

                @block.vector
                def _(vector, p=p, qh=qh, mmn=mm_bc0):
                    vector.wait_ge(s_mm.h, mmn)
                    s_v.inc(vector.reciprocal(rrec[0:1, :],
                                              p_b[64:65, 1024:2048]))
                v_rc1 = s_v.n

                @block.tensor
                def _(tensor, vs=s_v.n):
                    tensor.wait_ge(s_v.h, vs)
                    for c in range(2):
                        ins = tensor.matmul(
                            p_a[0:64, 1024 + c * 512:1024 + (c + 1) * 512],
                            onesf[0:1, 0:64],
                            rrec[0:1, c * 512:(c + 1) * 512],
                            start=True, stop=True)
                        if c == 1:
                            s_mm.inc(ins)
                mm_bc1 = s_mm.n

                @block.vector
                def _(vector, p=p, qh=qh, mmn=s_mm.n):
                    vector.wait_ge(s_mm.h, mmn)
                    if sh_cnt[0] > 0:
                        vector.wait_ge(s_sh.h, 16 * sh_cnt[0])
                    vector.tensor_tensor(aev2[0:64, :], aev2[0:64, :],
                                         p_a[0:64, 1024:2048], OP.mult)
                    vector.tensor_copy(oddst[0:64, 0:1024], aev2[0:64, :])
                    s_v.inc(vector.tensor_sub(oddst[0:64, 1024:2048],
                                              aev2[0:64, :],
                                              oddst[0:64, 0:1024]))
                avev[(p, qh)] = s_v.n

                @block.sync
                def _(sync, p=p, qh=qh, vs=s_v.n):
                    sync.wait_ge(s_v.h, vs)
                    dst = p * S + qh * 1024
                    s_sh.inc(sync.dma_start(out=aoT_hi[64:128, dst:dst + 1024],
                                            in_=oddst[0:64, 0:1024]), 16)
                    s_sh.inc(sync.dma_start(out=aoT_lo[64:128, dst:dst + 1024],
                                            in_=oddst[0:64, 1024:2048]), 16)
                sh_cnt[0] += 2
                if qh == 0:
                    avev[(p, 0)] = avev[(p, 0)]

            band_done[p] = vband[(p, 1, NT - 1)]

        h2mm = persist("h2mm", [128, 2], F32)
        gmm2 = persist("gmm2", [128, 2], F32)
        hmm2r = persist("hmm2r", [128, 2], F32)
        gmm2r = persist("gmm2r", [128, 2], F32)
        qv1 = persist("qv1", [128, 8], F32)
        qv2 = persist("qv2", [128, 8], F32)
        smi = persist("smi", [1, 8], I32)
        wtmp4 = persist("wtmp4", [128, 4], F32)
        C_MM = s_mm.n
        C_V = s_v.n
        C_SC = s_sc.n
        C_SH = 16 * sh_cnt[0]

        ctaps = [("qTh", qTh, [128, S], BF16), ("qTl", qTl, [128, S], BF16),
                 ("kTh", kTh, [128, S], BF16), ("kTl", kTl, [128, S], BF16),
                 ("v_hi", v_hi, [128, NT * 2 * 65], BF16),
                 ("v_lo", v_lo, [128, NT * 2 * 64], BF16),
                 ("ef32a", ef32a, [128, 1024], F32),
                 ("ef32b", ef32b, [128, 1024], F32),
                 ("aev", aev, [128, 1024], F32),
                 ("aev2", aev2, [128, 1024], F32),
                 ("rep0", rep0, [128, 382], F32),
                 ("rep1", rep1, [128, 382], F32),
                 ("pvec", pvec, [128, 4], F32),
                 ("expdFt", expdFt, [16, 512], F32),
                 ("dstat", dstat, [16, 4], F32)]
        for nm, src, shp, dt_ in ctaps:
            if tap(nm, shp, dt_) is not None:
                @block.sync
                def _(sync, nm=nm, src=src):
                    sync.wait_ge(s_v.h, C_V)
                    sync.wait_ge(s_sh.h, C_SH)
                    s_st.inc(sync.dma_start(out=dbg_outs[nm],
                                            in_=src[:, :]), 16)
        C_TAPS = s_st.n

        if tap("aoT_hi", [128, ND * S], BF16) is not None:
            @block.sync
            def _(sync):
                sync.wait_ge(s_v.h, C_V)
                sync.wait_ge(s_sh.h, C_SH)
                s_st.inc(sync.dma_start(out=dbg_outs["aoT_hi"],
                                        in_=aoT_hi[:, :]), 16)
        if tap("aoT_lo", [128, ND * S], BF16) is not None:
            @block.sync
            def _(sync):
                sync.wait_ge(s_v.h, C_V)
                sync.wait_ge(s_sh.h, C_SH)
                s_st.inc(sync.dma_start(out=dbg_outs["aoT_lo"],
                                        in_=aoT_lo[:, :]), 16)

        @block.sync
        def _(sync, ct=None):
            sync.wait_ge(s_mm.h, C_MM)
            sync.wait_ge(s_v.h, C_V)
            sync.wait_ge(s_sh.h, C_SH)
            sync.wait_ge(s_pk.h, 16)
            if C_TAPS:
                sync.wait_ge(s_st.h, C_TAPS)
            s_xr.inc(sync.dma_start(out=xT[:, :], in_=xT_park), 16)

        @block.vector
        def _(vector):
            vector.memset(h2mm[:, :], -3.0e38)
            s_v.inc(vector.memset(gmm2[:, :], -3.0e38))

        for j in range(ND):
            @block.sync
            def _(sync, j=j):
                if j >= 2:
                    sync.wait_ge(s_v.h, wospl[j - 2])
                sem = s_wo0 if j % 2 == 0 else s_wo1
                sem.inc(sync.dma_start(
                    out=wosl[:, (j % 2) * D:(j % 2 + 1) * D],
                    in_=wo_in[j * 128:(j + 1) * 128, :]), 16)

            @block.scalar
            def _(scalar, j=j):
                sem = s_wo0 if j % 2 == 0 else s_wo1
                scalar.wait_ge(sem.h, 16 * (j // 2 + 1))
                if j == 0:
                    scalar.wait_ge(s_mm.h, C_MM)
                s_sc.inc(scalar.copy(wo_hi[:, j * D:(j + 1) * D],
                                     wosl[:, (j % 2) * D:(j % 2 + 1) * D]))

            @block.vector
            def _(vector, j=j, scn=s_sc.n):
                vector.wait_ge(s_sc.h, scn)
                s_v.inc(vector.tensor_sub(
                    wo_lo[:, j * D:(j + 1) * D],
                    wosl[:, (j % 2) * D:(j % 2 + 1) * D],
                    wo_hi[:, j * D:(j + 1) * D]))
            wospl.append(s_v.n)

        for m in range(ND):
            @block.tensor
            def _(tensor, m=m):
                if m == 0:
                    tensor.wait_ge(s_v.h, wospl[-1])
                    tensor.wait_ge(s_sh.h, C_SH)
                buf = p_a if m % 2 == 0 else p_b
                if m >= 2:
                    tensor.wait_ge(s_v.h, x2ev[m - 2])
                n = 0
                for pi, (wl, al) in enumerate([(0, 0), (0, 1), (1, 0)]):
                    wsrc = wo_lo if wl else wo_hi
                    asrc = aoT_lo if al else aoT_hi
                    for kk in range(ND):
                        for c in range(4):
                            n += 1
                            ins = tensor.matmul(
                                buf[:, c * 512:(c + 1) * 512],
                                wsrc[:, kk * D + m * 128:kk * D + (m + 1) * 128],
                                asrc[:, kk * S + c * 512:kk * S + (c + 1) * 512],
                                start=(pi == 0 and kk == 0),
                                stop=(pi == 2 and kk == ND - 1))
                            if n == 96:
                                s_mm.inc(ins)

            @block.vector
            def _(vector, m=m, mmn=s_mm.n):
                vector.wait_ge(s_mm.h, mmn)
                if m == 0:
                    vector.wait_ge(s_xr.h, 16)
                buf = p_a if m % 2 == 0 else p_b
                s_v.inc(vector.scalar_tensor_tensor(
                    xT[:, m * S:(m + 1) * S], buf[:, :],
                    modT[:, 16 + m:17 + m], xT[:, m * S:(m + 1) * S],
                    OP.mult, OP.add))
            x2ev.append(s_v.n)
        V_X2 = s_v.n

        @block.sync
        def _(sync):
            sync.wait_ge(s_v.h, V_X2)
            s_x2s.inc(sync.dma_start(out=x2T_park, in_=xT[:, :]), 16)

        if tap("x2T", [128, ND * S]) is not None:
            @block.sync
            def _(sync):
                sync.wait_ge(s_v.h, V_X2)
                s_st.inc(sync.dma_start(out=dbg_outs["x2T"], in_=xT[:, :]), 16)

        MM_LN2_0 = s_mm.n

        @block.scalar
        def _(scalar):
            for d in range(ND):
                scalar.wait_ge(s_v.h, x2ev[d])
                if d >= 1:
                    scalar.wait_ge(s_mm.h, MM_LN2_0 + d)
                s_sc.inc(scalar.activation(ln2sc[:, :], xT[:, d * S:(d + 1) * S],
                                           AF.Square))
        SC_SQ2 = s_sc.n - ND

        @block.tensor
        def _(tensor):
            tensor.wait_ge(s_v.h, V_X2)
            for d in range(ND):
                for c in range(4):
                    tensor.matmul(p_a[0:1, c * 512:(c + 1) * 512], onesf[:, 0:1],
                                  xT[:, d * S + c * 512:d * S + (c + 1) * 512],
                                  start=(d == 0), stop=(d == ND - 1),
                                  skip_group_check=True)
                tensor.wait_ge(s_sc.h, SC_SQ2 + d + 1)
                for c in range(4):
                    ins = tensor.matmul(p_b[0:1, c * 512:(c + 1) * 512],
                                        onesf[:, 0:1],
                                        ln2sc[:, c * 512:(c + 1) * 512],
                                        start=(d == 0), stop=(d == ND - 1),
                                        skip_group_check=True)
                    if c == 3:
                        s_mm.inc(ins)
        MM_LN2 = s_mm.n

        @block.vector
        def _(vector):
            vector.wait_ge(s_mm.h, MM_LN2)
            vector.tensor_scalar(mean2_bc[0:1, :], p_a[0:1, :], 1.0 / D, None,
                                 OP.mult)
            vector.tensor_scalar(rstd2_bc[0:1, :], p_b[0:1, :], 1.0 / D, None,
                                 OP.mult)
            vector.tensor_tensor(ln2sc[0:1, :], mean2_bc[0:1, :],
                                 mean2_bc[0:1, :], OP.mult)
            s_v.inc(vector.tensor_tensor(rstd2_bc[0:1, :], rstd2_bc[0:1, :],
                                         ln2sc[0:1, :], OP.subtract))
        V_LN2V = s_v.n

        @block.scalar
        def _(scalar):
            scalar.wait_ge(s_v.h, V_LN2V)
            scalar.activation(ln2sc[0:1, :], rstd2_bc[0:1, :], AF.Ln,
                              bias=epsv[0:1, :])
            s_sc.inc(scalar.activation(rstd2_bc[0:1, :], ln2sc[0:1, :], AF.Exp,
                                       scale=-0.5))
        SC_LN2R = s_sc.n

        @block.tensor
        def _(tensor):
            tensor.wait_ge(s_sc.h, SC_LN2R)
            for c in range(4):
                ins = tensor.matmul(p_a[:, c * 512:(c + 1) * 512], onesf[0:1, :],
                                    mean2_bc[0:1, c * 512:(c + 1) * 512],
                                    start=True, stop=True)
                if c == 3:
                    s_mm.inc(ins)
            for c in range(4):
                ins = tensor.matmul(p_b[:, c * 512:(c + 1) * 512], onesf[0:1, :],
                                    rstd2_bc[0:1, c * 512:(c + 1) * 512],
                                    start=True, stop=True)
                if c == 3:
                    s_mm.inc(ins)
        MM_LN2B = s_mm.n

        @block.vector
        def _(vector):
            vector.wait_ge(s_mm.h, MM_LN2B - 1)
            vector.tensor_copy(mean2_bc[:, :], p_a[:, :])
            vector.wait_ge(s_mm.h, MM_LN2B)
            s_v.inc(vector.tensor_copy(rstd2_bc[:, :], p_b[:, :]))
        V_LN2B = s_v.n

        @block.vector
        def _(vector):
            vector.wait_ge(s_v.h, V_LN2B)
            for d in range(ND):
                vector.tensor_sub(h2T[:, d * S:(d + 1) * S],
                                  xT[:, d * S:(d + 1) * S], mean2_bc[:, :])
                vector.tensor_mul(h2T[:, d * S:(d + 1) * S],
                                  h2T[:, d * S:(d + 1) * S], rstd2_bc[:, :])
                vector.tensor_scalar(h2T[:, d * S:(d + 1) * S],
                                     h2T[:, d * S:(d + 1) * S],
                                     modT[:, 32 + d:33 + d],
                                     modT[:, 24 + d:25 + d], OP.mult, OP.add)
                fence(vector)
                vector.tensor_reduce(wred[:, :], h2T[:, d * S:(d + 1) * S],
                                     AX.X, OP.max)
                fence(vector)
                vector.tensor_tensor(h2mm[:, 0:1], h2mm[:, 0:1], wred[:, :],
                                     OP.max)
                vector.tensor_reduce(wred[:, :], h2T[:, d * S:(d + 1) * S],
                                     AX.X, OP.min, negate=True)
                fence(vector)
                s_v.inc(vector.tensor_tensor(h2mm[:, 1:2], h2mm[:, 1:2],
                                             wred[:, :], OP.max))
        V_H2 = s_v.n

        if tap("h2T", [128, ND * S]) is not None:
            @block.sync
            def _(sync):
                sync.wait_ge(s_v.h, V_H2)
                s_st.inc(sync.dma_start(out=dbg_outs["h2T"], in_=h2T[:, :]), 16)

        V_WA2_0 = s_v.n

        @block.vector
        def _(vector):
            for i in range(16):
                vector.wait_ge(s_wa.h, LD_WA1 + 16 * (i + 1))
                vector.tensor_reduce(wred[:, :], wabsD[:, :], AX.X, OP.max,
                                     apply_absolute_value=True)
                fence(vector)
                s_v.inc(vector.tensor_tensor(wmax2[:, :], wmax2[:, :],
                                             wred[:, :], OP.max))
        V_WA2 = s_v.n

        @block.sync
        def _(sync):
            sync.wait_ge(s_mm.h, MM_LN2)
            for i in range(16):
                if i > 0:
                    sync.wait_ge(s_v.h, V_WA2_0 + i)
                s_wa.inc(sync.dma_start(
                    out=wabsD[:, :],
                    in_=bass.AP(w2_in.tensor, (i // 2) * 128 * FF +
                                (i % 2) * 2048,
                                [[FF, 128], [1, 2048]])), 16)

        @block.sync
        def _(sync):
            sync.wait_ge(s_v.h, V_H2)
            s_cs1.inc(sync.dma_start(out=cc1_i.ap(), in_=h2mm[:, :]), 16)

        @block.gpsimd
        def _(gpsimd):
            gpsimd.wait_ge(s_cs1.h, 16)
            s_gp.inc(gpsimd.collective_compute(
                "AllReduce", OP.max, replica_groups=[list(range(ranks))],
                ins=[cc1_i.ap().opt()], outs=[cc1_o.ap().opt()]))
            gpsimd.wait_ge(s_gp.h, s_gp.n)
            s_cr1.inc(gpsimd.dma_start(out=hmm2r[:, :], in_=cc1_o.ap()), 16)
        GP_AR1 = s_gp.n
        CS1_BACK = s_cr1.n

        @block.sync
        def _(sync):
            sync.wait_ge(s_cr1.h, CS1_BACK)
            sync.wait_ge(s_v.h, V_WA1)
            sync.wait_ge(s_v.h, V_WA2)
            s_cp.inc(sync.dma_start(
                out=grow[0:1, 0:128],
                in_=bass.AP(hmm2r, 0, [[2, 128], [1, 1]])), 16)
            s_cp.inc(sync.dma_start(
                out=grow[0:1, 128:256],
                in_=bass.AP(hmm2r, 1, [[2, 128], [1, 1]])), 16)
            s_cp.inc(sync.dma_start(out=grow[0:1, 256:384],
                                    in_=wmax1[:, 0:1]), 16)
            s_cp.inc(sync.dma_start(out=grow[0:1, 384:512],
                                    in_=wmax2[:, 0:1]), 16)
        CP_T1 = s_cp.n

        @block.vector
        def _(vector):
            vector.wait_ge(s_cp.h, CP_T1)
            vector.tensor_reduce(sm1[0:1, 0:1], grow[0:1, 0:128], AX.X, OP.max)
            vector.tensor_reduce(sm1[0:1, 1:2], grow[0:1, 128:256], AX.X,
                                 OP.max)
            vector.tensor_reduce(sm1[0:1, 13:14], grow[0:1, 384:512], AX.X,
                                 OP.max)
            fence(vector)
            vector.tensor_scalar(sm1[0:1, 13:14], sm1[0:1, 13:14], 1.0 / 127.0,
                                 1e-8, OP.mult, OP.max)
            fence(vector)
            vector.reciprocal(sm1[0:1, 14:15], sm1[0:1, 13:14])
            s_v.inc(vector.tensor_reduce(sm1[0:1, 2:3], grow[0:1, 256:384],
                                         AX.X, OP.max))
        V_AR1X = s_v.n

        @block.tensor
        def _(tensor):
            tensor.wait_ge(s_v.h, V_AR1X)
            s_mm.inc(tensor.matmul(p_a[:, 1031:1032], onesf[0:1, :],
                                   sm1[0:1, 14:15], start=True, stop=True,
                                   skip_group_check=True))
        MM_ISW2 = s_mm.n

        @block.vector
        def _(vector):
            vector.wait_ge(s_mm.h, MM_ISW2)
            s_v.inc(vector.tensor_tensor(qv2[:, 5:6], p_a[:, 1031:1032],
                                         sv[:, 0:1], OP.add))
        V_ISW2 = s_v.n

        def emit_scale_chain(base_v, which):
            qv = qv1 if which == 1 else qv2

            @block.vector
            def _(vector, base_v=base_v):
                vector.wait_ge(s_v.h, base_v)
                r = sm1
                fence(vector)
                vector.tensor_tensor(r[0:1, 3:4], r[0:1, 0:1], r[0:1, 1:2],
                                     OP.add)
                vector.tensor_scalar(r[0:1, 5:6], r[0:1, 2:3], 1.0 / 127.0,
                                     1e-8, OP.mult, OP.max)
                fence(vector, 2)
                vector.tensor_scalar(r[0:1, 3:4], r[0:1, 3:4], 1.0 / 255.0,
                                     1e-8, OP.mult, OP.max)
                vector.reciprocal(r[0:1, 6:7], r[0:1, 5:6])
                fence(vector, 2)
                vector.reciprocal(r[0:1, 4:5], r[0:1, 3:4])
                fence(vector)
                vector.tensor_tensor(r[0:1, 7:8], r[0:1, 1:2], r[0:1, 4:5],
                                     OP.mult)
                fence(vector)
                s_v.inc(vector.tensor_scalar(r[0:1, 7:8], r[0:1, 7:8], 1.0,
                                             RK, OP.mult, OP.add))
            bv = s_v.n

            @block.scalar
            def _(scalar, bv=bv):
                scalar.wait_ge(s_v.h, bv)
                scalar.copy(smi[0:1, 0:1], sm1[0:1, 7:8])
                s_sc.inc(scalar.copy(sm1[0:1, 8:9], smi[0:1, 0:1]))
            bs = s_sc.n

            @block.vector
            def _(vector, bs=bs):
                vector.wait_ge(s_sc.h, bs)
                r = sm1
                vector.tensor_tensor(r[0:1, 9:10], r[0:1, 7:8], r[0:1, 8:9],
                                     OP.subtract)
                vector.tensor_tensor(r[0:1, 11:12], r[0:1, 3:4], r[0:1, 5:6],
                                     OP.mult)
                fence(vector, 2)
                vector.scalar_tensor_tensor(r[0:1, 7:8], r[0:1, 9:10], 0.5,
                                            r[0:1, 8:9], OP.is_ge, OP.add)
                fence(vector)
                vector.tensor_scalar(r[0:1, 7:8], r[0:1, 7:8], RK - QMIN,
                                     QMAX, OP.subtract, OP.min)
                fence(vector)
                vector.tensor_scalar(r[0:1, 7:8], r[0:1, 7:8], QMIN, None,
                                     OP.max)
                fence(vector)
                vector.scalar_tensor_tensor(r[0:1, 10:11], r[0:1, 7:8], -1.0,
                                            sv[0:1, 0:1], OP.mult, OP.add)
                fence(vector)
                vector.tensor_scalar(r[0:1, 10:11], r[0:1, 10:11], RK, None,
                                     OP.add)
                fence(vector)
                vector.tensor_scalar(r[0:1, 8:9], r[0:1, 10:11], QMAX, None,
                                     OP.add)
                vector.tensor_scalar(r[0:1, 9:10], r[0:1, 10:11], QMIN, None,
                                     OP.add)
                s_v.inc(vector.tensor_tensor(r[0:1, 12:13], r[0:1, 7:8],
                                             r[0:1, 11:12], OP.mult))
            bv2 = s_v.n

            @block.tensor
            def _(tensor, bv2=bv2):
                tensor.wait_ge(s_v.h, bv2)
                for j, col in enumerate([4, 8, 9, 10, 11, 6, 12]):
                    ins = tensor.matmul(p_a[:, 1024 + j:1025 + j], onesf[0:1, :],
                                        sm1[0:1, col:col + 1],
                                        start=True, stop=True,
                                        skip_group_check=True)
                    if j == 6:
                        s_mm.inc(ins)
            bmm = s_mm.n

            @block.vector
            def _(vector, bmm=bmm, qv=qv):
                vector.wait_ge(s_mm.h, bmm)
                s_v.inc(vector.tensor_copy(qv[:, 0:7], p_a[:, 1024:1031]))
            return s_v.n

        V_QV1 = emit_scale_chain(V_AR1X, 1)

        for d in range(ND):
            @block.vector
            def _(vector, d=d):
                if d == 0:
                    vector.wait_ge(s_v.h, V_QV1)
                    vector.wait_ge(s_x2s.h, 16)
                vector.wait_ge(s_sc.h, eqsc[d - 1] if d else 0)
                s_v.inc(vector.tensor_scalar(eq_u[:, :], h2T[:, d * S:(d + 1) * S],
                                             qv1[:, 0:1], RK, OP.mult, OP.add))
            equ = s_v.n

            @block.scalar
            def _(scalar, d=d, equ=equ):
                scalar.wait_ge(s_v.h, equ)
                if d >= 1:
                    scalar.wait_ge(s_gp.h, eqgpl[d - 1])
                scalar.copy(eq_i32[:, :], eq_u[:, :])
                s_sc.inc(scalar.copy(eq_if[:, :], eq_i32[:, :]))
            eqsc.append(s_sc.n)

            @block.gpsimd
            def _(gpsimd, d=d, scn=s_sc.n):
                gpsimd.wait_ge(s_sc.h, scn)
                s_gp.inc(gpsimd.tensor_sub(eq_d[:, :], eq_u[:, :], eq_if[:, :]))
            eqgpl.append(s_gp.n)

            @block.vector
            def _(vector, d=d, gpn=s_gp.n):
                vector.wait_ge(s_gp.h, gpn)
                vector.scalar_tensor_tensor(eq_u[:, :], eq_d[:, :], 0.5,
                                            eq_if[:, :], OP.is_ge, OP.add)
                vector.tensor_scalar(eq_u[:, :], eq_u[:, :], qv1[:, 1:2],
                                     qv1[:, 2:3], OP.min, OP.max)
                fence(vector)
                s_v.inc(vector.tensor_scalar(xqT[:, d * S:(d + 1) * S],
                                             eq_u[:, :], qv1[:, 3:4], None,
                                             OP.subtract))
            eqv.append(s_v.n)
        V_XQ = s_v.n

        if tap("xqT", [128, ND * S], BF16) is not None:
            @block.sync
            def _(sync):
                sync.wait_ge(s_v.h, V_XQ)
                s_st.inc(sync.dma_start(out=dbg_outs["xqT"], in_=xqT[:, :]), 16)

        for si in range(8):
            @block.sync
            def _(sync, si=si):
                if si == 0:
                    sync.wait_ge(s_v.h, V_QV1)
                    sync.wait_ge(s_x2s.h, 16)
                else:
                    sync.wait_ge(s_v.h, w1q[si - 1])
                sem = s_w1n0 if si % 2 == 0 else s_w1n1
                sem.inc(sync.dma_start(
                    out=w1sl[:, :],
                    in_=bass.AP(w1_in.tensor, si * 512 * D,
                                [[D, 128], [128 * D, 4], [1, D]])), 16)

            @block.vector
            def _(vector, si=si):
                sem = s_w1n0 if si % 2 == 0 else s_w1n1
                vector.wait_ge(sem.h, 16 * (si // 2 + 1))
                if si >= 1:
                    vector.wait_ge(s_sc.h, w1sc[si - 1])
                s_v.inc(vector.tensor_scalar(fsc[:, 0:S], w1sl[:, 0:S],
                                             qv1[:, 5:6], RK, OP.mult, OP.add))
                s_v.inc(vector.tensor_scalar(fsc2[:, 0:S], w1sl[:, S:2 * S],
                                             qv1[:, 5:6], RK, OP.mult, OP.add))
            w1v1 = s_v.n

            @block.scalar
            def _(scalar, si=si, vn=s_v.n):
                scalar.wait_ge(s_v.h, vn)
                scalar.copy(fsi[:, 0:S], fsc[:, 0:S])
                scalar.copy(w2sl[:, 0:S], fsi[:, 0:S])
                scalar.copy(fsi[:, 0:S], fsc2[:, 0:S])
                s_sc.inc(scalar.copy(w2sl[:, S:2 * S], fsi[:, 0:S]))
            w1sc.append(s_sc.n)

            @block.vector
            def _(vector, si=si, scn=s_sc.n):
                vector.wait_ge(s_sc.h, scn)
                for (u, i) in ((fsc, 0), (fsc2, 1)):
                    iv = w2sl[:, i * S:(i + 1) * S]
                    vector.tensor_sub(u[:, :], u[:, :], iv)
                    vector.scalar_tensor_tensor(u[:, :], u[:, :], 0.5,
                                                iv, OP.is_ge, OP.add)
                    vector.tensor_scalar(u[:, :], u[:, :], RK + 127.0,
                                         RK - 127.0, OP.min, OP.max)
                vector.tensor_scalar(wq1nb[:, 0:S], fsc[:, :], RK, None,
                                     OP.subtract)
                s_v.inc(vector.tensor_scalar(wq1nb[:, S:2 * S], fsc2[:, :],
                                             RK, None, OP.subtract))
                fence(vector)
                for j in range(4):
                    vector.tensor_reduce(
                        w1sT[:, si * 4 + j:si * 4 + j + 1],
                        wq1nb[:, j * D:(j + 1) * D], AX.X, OP.add)
                fence(vector)
                vector.tensor_scalar(wtmp4[:, :], w1sT[:, si * 4:si * 4 + 4],
                                     qv1[:, 6:7], None, OP.mult)
                fence(vector)
                s_v.inc(vector.tensor_tensor(bb1T[:, si * 4:si * 4 + 4],
                                             b1T[:, si * 4:si * 4 + 4],
                                             wtmp4[:, :], OP.subtract))
            w1q.append(s_v.n)

            @block.sync
            def _(sync, si=si, vn=s_v.n):
                sync.wait_ge(s_v.h, vn)
                s_w1s.inc(sync.dma_start(
                    out=bass.AP(wq1n_dram.tensor, si * 512 * D,
                                [[D, 128], [128 * D, 4], [1, D]]),
                    in_=wq1nb[:, :]), 16)

            @block.scalar
            def _(scalar, si=si, ws=s_w1s.n, xq=V_XQ):
                scalar.wait_ge(s_w1s.h, ws)
                if si == 0:
                    scalar.wait_ge(s_v.h, xq)
                else:
                    scalar.wait_ge(s_mm.h, w1mm[si - 1])
                for dd in range(ND):
                    s_w1t.inc(scalar.dma_start(
                        out=wq1T_full[:, dd * 512:(dd + 1) * 512],
                        in_=bass.AP(wq1n_dram.tensor, si * 512 * D + dd * 128,
                                    [[D, 512], [1, 128]]),
                        transpose=True), 16)

            for mloc in range(4):
                @block.tensor
                def _(tensor, si=si, mloc=mloc, wt=s_w1t.n, xq=V_XQ):
                    tensor.wait_ge(s_w1t.h, wt)
                    if si == 0 and mloc == 0:
                        tensor.wait_ge(s_v.h, xq)
                    buf = p_a if mloc % 2 == 0 else p_b
                    if len(g_ev) >= 2:
                        tensor.wait_ge(s_v.h, g_ev[-2])
                    n = 0
                    for kk in range(ND):
                        for c in range(4):
                            n += 1
                            ins = tensor.matmul(
                                buf[:, c * 512:(c + 1) * 512],
                                wq1T_full[:, kk * 512 + mloc * 128:
                                          kk * 512 + mloc * 128 + 128],
                                xqT[:, kk * S + c * 512:kk * S + (c + 1) * 512],
                                start=(kk == 0), stop=(kk == ND - 1))
                            if n == 32:
                                s_mm.inc(ins)
                g_mm.append(s_mm.n)

                @block.vector
                def _(vector, si=si, mloc=mloc, mmn=s_mm.n,
                      prevg=(gsc_l[-1] if gsc_l else 0)):
                    vector.wait_ge(s_mm.h, mmn)
                    if prevg:
                        vector.wait_ge(s_sc.h, prevg)
                    f = si * 4 + mloc
                    buf = p_a if mloc % 2 == 0 else p_b
                    s_v.inc(vector.tensor_scalar(gbuf[:, :], buf[:, :],
                                                 qv1[:, 4:5], bb1T[:, f:f + 1],
                                                 OP.mult, OP.add))
                gpre = s_v.n

                @block.scalar
                def _(scalar, vn=s_v.n, prevsp=len(g_ev)):
                    scalar.wait_ge(s_v.h, vn)
                    if prevsp:
                        scalar.wait_ge(s_gs.h, 16 * prevsp)
                    s_sc.inc(scalar.activation(gbuf2[:, :], gbuf[:, :], AF.Gelu))
                gsc_l.append(s_sc.n)

                @block.vector
                def _(vector, scn=s_sc.n, gs=len(g_ev)):
                    vector.wait_ge(s_sc.h, scn)
                    if gs >= 1:
                        vector.wait_ge(s_gs.h, 16 * gs)
                    vector.tensor_reduce(wred[:, :], gbuf2[:, :], AX.X, OP.max)
                    fence(vector)
                    vector.tensor_tensor(gmm2[:, 0:1], gmm2[:, 0:1], wred[:, :],
                                         OP.max)
                    vector.tensor_reduce(wred[:, :], gbuf2[:, :], AX.X, OP.min,
                                         negate=True)
                    fence(vector)
                    s_v.inc(vector.tensor_tensor(gmm2[:, 1:2], gmm2[:, 1:2],
                                                 wred[:, :], OP.max))
                g_ev.append(s_v.n)

                @block.sync
                def _(sync, si=si, mloc=mloc, vn=s_v.n):
                    sync.wait_ge(s_v.h, vn)
                    f = si * 4 + mloc
                    s_gs.inc(sync.dma_start(
                        out=bass.AP(g_spill.tensor, f * 128 * S,
                                    [[S, 128], [1, S]]),
                        in_=gbuf2[:, :]), 16)
            w1mm.append(s_mm.n)
        MM_FFN1 = s_mm.n
        V_FFN1 = s_v.n

        w2sc = []
        w2q = []
        for ci in range(16):
            @block.sync
            def _(sync, ci=ci):
                if ci == 0:
                    sync.wait_ge(s_v.h, V_FFN1)
                else:
                    sync.wait_ge(s_v.h, w2q[ci - 1])
                sem = s_w2n if ci % 2 == 0 else s_w2s
                sem.inc(sync.dma_start(
                    out=w2sl[:, 0:2048],
                    in_=bass.AP(w2_in.tensor,
                                (ci // 2) * 128 * FF + (ci % 2) * 2048,
                                [[FF, 128], [1, 2048]])), 16)

            @block.vector
            def _(vector, ci=ci):
                sem = s_w2n if ci % 2 == 0 else s_w2s
                vector.wait_ge(sem.h, 16 * (ci // 2 + 1))
                if ci == 0:
                    vector.wait_ge(s_v.h, V_ISW2)
                if ci >= 1:
                    vector.wait_ge(s_sc.h, w2sc[ci - 1])
                s_v.inc(vector.tensor_scalar(fsc[:, :], w2sl[:, 0:2048],
                                             qv2[:, 5:6], RK, OP.mult, OP.add))
            w2v = s_v.n

            @block.scalar
            def _(scalar, vn=s_v.n):
                scalar.wait_ge(s_v.h, vn)
                scalar.copy(fsi[:, :], fsc[:, :])
                s_sc.inc(scalar.copy(fsc2[:, :], fsi[:, :]))
            w2sc.append(s_sc.n)

            @block.vector
            def _(vector, ci=ci, scn=s_sc.n):
                vector.wait_ge(s_sc.h, scn)
                vector.tensor_sub(fsc[:, :], fsc[:, :], fsc2[:, :])
                vector.scalar_tensor_tensor(fsc[:, :], fsc[:, :], 0.5,
                                            fsc2[:, :], OP.is_ge, OP.add)
                vector.tensor_scalar(fsc[:, :], fsc[:, :], RK + 127.0,
                                     RK - 127.0, OP.min, OP.max)
                vector.tensor_scalar(wq2nb[:, 0:2048], fsc[:, :], RK, None,
                                     OP.subtract)
                fence(vector)
                vector.tensor_reduce(wred[:, :], wq2nb[:, 0:2048], AX.X, OP.add)
                fence(vector)
                if ci % 2 == 0:
                    s_v.inc(vector.tensor_tensor(wtmp4[:, 0:1], wred[:, :],
                                                 sv[:, 0:1], OP.add))
                else:
                    s_v.inc(vector.tensor_tensor(
                        w2sT[:, ci // 2:ci // 2 + 1], wtmp4[:, 0:1],
                        wred[:, :], OP.add))
            w2q.append(s_v.n)

            @block.sync
            def _(sync, ci=ci, vn=s_v.n):
                sync.wait_ge(s_v.h, vn)
                s_w1s.inc(sync.dma_start(
                    out=bass.AP(wq2n_dram.tensor,
                                (ci // 2) * 128 * FF + (ci % 2) * 2048,
                                [[FF, 128], [1, 2048]]),
                    in_=wq2nb[:, 0:2048]), 16)
        W2N_ST = s_w1s.n

        @block.scalar
        def _(scalar):
            scalar.wait_ge(s_w1s.h, W2N_ST)
            for kt in range(NF):
                s_w2t.inc(scalar.dma_start(
                    out=wq2T[:, kt * D:(kt + 1) * D],
                    in_=bass.AP(wq2n_dram.tensor, kt * 128,
                                [[FF, D], [1, 128]]),
                    transpose=True), 16)
        W2T_LD = s_w2t.n

        if tap("wq2Te", [128, NF * D], BF16) is not None:
            @block.sync
            def _(sync):
                sync.wait_ge(s_w2t.h, W2T_LD)
                s_st.inc(sync.dma_start(out=dbg_outs["wq2Te"],
                                        in_=wq2T[:, :]), 16)

        @block.sync
        def _(sync):
            sync.wait_ge(s_v.h, V_FFN1)
            s_cs2.inc(sync.dma_start(out=cc2_i.ap(), in_=gmm2[:, :]), 16)

        @block.gpsimd
        def _(gpsimd):
            gpsimd.wait_ge(s_cs2.h, 16)
            s_gp.inc(gpsimd.collective_compute(
                "AllReduce", OP.max, replica_groups=[list(range(ranks))],
                ins=[cc2_i.ap().opt()], outs=[cc2_o.ap().opt()]))
            gpsimd.wait_ge(s_gp.h, s_gp.n)
            s_cr2.inc(gpsimd.dma_start(out=gmm2r[:, :], in_=cc2_o.ap()), 16)
        CS2_BACK = s_cr2.n

        @block.sync
        def _(sync):
            sync.wait_ge(s_cr2.h, CS2_BACK)
            sync.wait_ge(s_v.h, V_WA2)
            s_cp.inc(sync.dma_start(
                out=grow[0:1, 0:128],
                in_=bass.AP(gmm2r, 0, [[2, 128], [1, 1]])), 16)
            s_cp.inc(sync.dma_start(
                out=grow[0:1, 128:256],
                in_=bass.AP(gmm2r, 1, [[2, 128], [1, 1]])), 16)
            s_cp.inc(sync.dma_start(out=grow[0:1, 256:384],
                                    in_=wmax2[:, 0:1]), 16)
        CP_T2 = s_cp.n

        @block.vector
        def _(vector):
            vector.wait_ge(s_cp.h, CP_T2)
            vector.tensor_reduce(sm1[0:1, 0:1], grow[0:1, 0:128], AX.X, OP.max)
            vector.tensor_reduce(sm1[0:1, 1:2], grow[0:1, 128:256], AX.X,
                                 OP.max)
            s_v.inc(vector.tensor_reduce(sm1[0:1, 2:3], grow[0:1, 256:384],
                                         AX.X, OP.max))
        V_AR2X = s_v.n

        V_QV2 = emit_scale_chain(V_AR2X, 2)

        wtmp8 = persist("wtmp8", [128, 8], F32)

        @block.vector
        def _(vector):
            vector.wait_ge(s_v.h, V_QV2)
            vector.tensor_scalar(wtmp8[:, :], w2sT[:, 0:8], qv2[:, 6:7],
                                 None, OP.mult)
            fence(vector)
            s_v.inc(vector.tensor_tensor(bb2T[:, 0:8], b2T[:, 0:8],
                                         wtmp8[:, :], OP.subtract))
        V_BB2 = s_v.n

        gq_v = {}
        ystores = 0
        for tq in range(4):
            @block.sync
            def _(sync, tq=tq):
                if tq == 0:
                    sync.wait_ge(s_x2s.h, 16)
                    sync.wait_ge(s_v.h, V_BB2)
                    sync.wait_ge(s_w1s.h, W2N_ST)
                else:
                    sync.wait_ge(s_v.h, x3ev[(tq - 1, ND - 1)])
                for m in range(ND):
                    s_x2r.inc(sync.dma_start(
                        out=x2q[:, m * 512:(m + 1) * 512],
                        in_=bass.AP(x2T_park.tensor,
                                    m * S + tq * 512,
                                    [[ND * S, 128], [1, 512]])), 16)

            for kt in range(NF):
                @block.sync
                def _(sync, tq=tq, kt=kt):
                    gi = tq * NF + kt
                    if gi == 0:
                        sync.wait_ge(s_gs.h, 16 * 32)
                        sync.wait_ge(s_v.h, V_QV2)
                    if gi >= 2:
                        sync.wait_ge(s_v.h, gq_v[gi - 2])
                    sem = s_gr0 if kt % 2 == 0 else s_gr1
                    sem.inc(sync.dma_start(
                        out=grel[:, (kt % 2) * S:(kt % 2) * S + 512],
                        in_=bass.AP(g_spill.tensor, kt * 128 * S + tq * 512,
                                    [[S, 128], [1, 512]])), 16)

                @block.vector
                def _(vector, tq=tq, kt=kt,
                      prevy=(yev.get((tq - 1, 3), 0))):
                    gi = tq * NF + kt
                    sem = s_gr0 if kt % 2 == 0 else s_gr1
                    vector.wait_ge(sem.h, 16 * (gi // 2 + 1))
                    if gi >= 1:
                        vector.wait_ge(s_sc.h, gqsc[gi - 1])
                    if kt == 0 and prevy:
                        vector.wait_ge(s_v.h, prevy)
                    s_v.inc(vector.tensor_scalar(
                        ffsc[:, 0:512], grel[:, (kt % 2) * S:(kt % 2) * S + 512],
                        qv2[:, 0:1], RK, OP.mult, OP.add))
                gqu = s_v.n

                @block.scalar
                def _(scalar, vn=s_v.n, gi=tq * NF + kt, kt=kt,
                      y0=sy_tot[0], y1=sy_tot[1]):
                    scalar.wait_ge(s_v.h, vn)
                    if gi >= 1:
                        scalar.wait_ge(s_gp.h, gqgp[gi - 1])
                    if kt == 0:
                        if y0:
                            scalar.wait_ge(s_y0.h, y0)
                        if y1:
                            scalar.wait_ge(s_y1.h, y1)
                    scalar.copy(ffsi[:, 0:512], ffsc[:, 0:512])
                    s_sc.inc(scalar.copy(ffsc2[:, 0:512], ffsi[:, 0:512]))
                gqsc.append(s_sc.n)

                @block.gpsimd
                def _(gpsimd, scn=s_sc.n):
                    gpsimd.wait_ge(s_sc.h, scn)
                    s_gp.inc(gpsimd.tensor_sub(ffsd[:, 0:512], ffsc[:, 0:512],
                                               ffsc2[:, 0:512]))
                gqgp.append(s_gp.n)

                @block.vector
                def _(vector, tq=tq, kt=kt, gpn=s_gp.n):
                    vector.wait_ge(s_gp.h, gpn)
                    vector.scalar_tensor_tensor(ffsc[:, 0:512], ffsd[:, 0:512],
                                                0.5, ffsc2[:, 0:512],
                                                OP.is_ge, OP.add)
                    vector.tensor_scalar(ffsc[:, 0:512], ffsc[:, 0:512],
                                         qv2[:, 1:2], qv2[:, 2:3],
                                         OP.min, OP.max)
                    fence(vector)
                    s_v.inc(vector.tensor_scalar(
                        gqT[:, kt * 512:(kt + 1) * 512], ffsc[:, 0:512],
                        qv2[:, 3:4], None, OP.subtract))
                gq_v[tq * NF + kt] = s_v.n

                @block.tensor
                def _(tensor, tq=tq, kt=kt, vn=s_v.n):
                    tensor.wait_ge(s_v.h, vn)
                    if kt == 0:
                        tensor.wait_ge(s_w2t.h, W2T_LD)
                        if tq > 0:
                            tensor.wait_ge(s_v.h, x3ev[(tq - 1, ND - 1)])
                            tensor.wait_ge(s_v.h, yev[(tq - 1, 3)])
                    for m in range(ND):
                        buf = p_a if m < 4 else p_b
                        reg = (m % 4) ^ 1 if m < 2 else m % 4
                        ins = tensor.matmul(
                            buf[:, reg * 512:(reg + 1) * 512],
                            wq2T[:, kt * D + m * 128:kt * D + (m + 1) * 128],
                            gqT[:, kt * 512:(kt + 1) * 512],
                            start=(kt == 0), stop=(kt == NF - 1))
                        if m == ND - 1:
                            s_mm.inc(ins)
            mm_f2 = s_mm.n

            if tq == 3 and tap("pf2", [128, 1024], F32) is not None:
                @block.vector
                def _(vector, mmn=s_mm.n):
                    vector.wait_ge(s_mm.h, mmn)
                    s_v.inc(vector.tensor_copy(fbuf2[:, :], p_a[:, 0:1024]))

                @block.sync
                def _(sync, vn=s_v.n):
                    sync.wait_ge(s_v.h, vn)
                    s_st.inc(sync.dma_start(out=dbg_outs["pf2"],
                                            in_=fbuf2[:, :]), 16)

            for m in range(ND):
                @block.vector
                def _(vector, tq=tq, m=m, mmn=mm_f2, xr=s_x2r.n):
                    vector.wait_ge(s_mm.h, mmn)
                    vector.wait_ge(s_x2r.h, xr)
                    buf = p_a if m < 4 else p_b
                    reg = (m % 4) ^ 1 if m < 2 else m % 4
                    vector.tensor_scalar(ffsd[:, 0:512],
                                         buf[:, reg * 512:(reg + 1) * 512],
                                         qv2[:, 4:5], bb2T[:, m:m + 1],
                                         OP.mult, OP.add)
                    fence(vector)
                    s_v.inc(vector.scalar_tensor_tensor(
                        x3q[:, m * 512:(m + 1) * 512], ffsd[:, 0:512],
                        modT[:, 40 + m:41 + m], x2q[:, m * 512:(m + 1) * 512],
                        OP.mult, OP.add))
                x3ev[(tq, m)] = s_v.n

            for j in range(4):
                @block.tensor
                def _(tensor, tq=tq, j=j):
                    tensor.wait_ge(s_v.h, x3ev[(tq, ND - 1)])
                    if j >= 2:
                        tensor.wait_ge(s_v.h, yev[(tq, j - 2)])
                    base = (j % 2) * 1024
                    for m in range(ND):
                        ins = tensor.transpose(
                            p_a[:, base + m * 128:base + (m + 1) * 128],
                            x3q[:, m * 512 + j * 128:m * 512 + (j + 1) * 128],
                            ident[:, :])
                        if m == ND - 1:
                            s_mm.inc(ins)
                mm_y = s_mm.n

                @block.vector
                def _(vector, tq=tq, j=j, mmn=s_mm.n,
                      prev=sy_tot[(tq * 4 + j) % 2]):
                    vector.wait_ge(s_mm.h, mmn)
                    if prev:
                        vector.wait_ge((s_y0 if (tq * 4 + j) % 2 == 0
                                        else s_y1).h, prev)
                    s_v.inc(vector.tensor_copy(
                        ystage[:, (j % 2) * D:(j % 2 + 1) * D],
                        p_a[:, (j % 2) * 1024:(j % 2) * 1024 + 1024]))
                yev[(tq, j)] = s_v.n

                @block.sync
                def _(sync, tq=tq, j=j, vn=s_v.n):
                    sync.wait_ge(s_v.h, vn)
                    r0 = (tq * 4 + j) * 128
                    sem = s_y0 if (tq * 4 + j) % 2 == 0 else s_y1
                    sem.inc(sync.dma_start(
                        out=y_out[r0:r0 + 128, :],
                        in_=ystage[:, (j % 2) * D:(j % 2 + 1) * D]), 16)
                sy_tot[(tq * 4 + j) % 2] += 16
                ystores += 1

        if tap("x3qx", [128, ND * 512], F32) is not None:
            @block.sync
            def _(sync, vfin=s_v.n):
                sync.wait_ge(s_v.h, vfin)
                s_st.inc(sync.dma_start(out=dbg_outs["x3qx"],
                                        in_=x3q[:, :]), 16)
        if tap("gqTx", [128, NF * 512], BF16) is not None:
            @block.sync
            def _(sync, vfin=s_v.n):
                sync.wait_ge(s_v.h, vfin)
                s_st.inc(sync.dma_start(out=dbg_outs["gqTx"],
                                        in_=gqT[:, :]), 16)
        if tap("wq2Tx", [128, NF * D], BF16) is not None:
            @block.sync
            def _(sync, vfin=s_v.n):
                sync.wait_ge(s_v.h, vfin)
                s_st.inc(sync.dma_start(out=dbg_outs["wq2Tx"],
                                        in_=wq2T[:, :]), 16)
        if tap("x2qx", [128, ND * 512], F32) is not None:
            @block.sync
            def _(sync, vfin=s_v.n):
                sync.wait_ge(s_v.h, vfin)
                s_st.inc(sync.dma_start(out=dbg_outs["x2qx"],
                                        in_=x2q[:, :]), 16)
        etaps = [("qv1", qv1, [128, 8]), ("qv2", qv2, [128, 8]),
                 ("b1Tx", b1T, [128, NF]), ("b2Tx", b2T, [128, ND]),
                 ("bb1T", bb1T, [128, NF]), ("bb2T", bb2T, [128, ND]),
                 ("w1sT", w1sT, [128, NF]), ("w2sT", w2sT, [128, ND]),
                 ("h2mm", h2mm, [128, 2]), ("gmm2x", gmm2, [128, 2]),
                 ("hmm2r", hmm2r, [128, 2]), ("gmm2r", gmm2r, [128, 2]),
                 ("sm1x", sm1, [1, 16])]
        for nm, src, shp in etaps:
            if tap(nm, shp, F32) is not None:
                @block.sync
                def _(sync, nm=nm, src=src, vfin=s_v.n):
                    sync.wait_ge(s_v.h, vfin)
                    s_st.inc(sync.dma_start(out=dbg_outs[nm],
                                            in_=src[:, :]), 16)

        @block.sync
        def _(sync):
            sync.wait_ge(s_y0.h, sy_tot[0])
            sync.wait_ge(s_y1.h, sy_tot[1])
            if s_st.n:
                sync.wait_ge(s_st.h, s_st.n)
            sync.wait_ge(s_v.h, s_v.n)
            sync.wait_ge(s_sc.h, s_sc.n)
            sync.wait_ge(s_mm.h, s_mm.n)
            sync.wait_ge(s_gp.h, s_gp.n)

    return nc, dbg_outs


def kernel(**inputs):
    cons = host_constants()
    nc, dbg_outs = build_program(dbg=())
    in_maps = []
    for b in range(8):
        m = {
            "x": np.ascontiguousarray(inputs["L0_attn_hidden_states"][b]),
            "c": np.ascontiguousarray(inputs["L0_c"][b:b + 1]),
            "wq": inputs["L0_attn_wq"], "wk": inputs["L0_attn_wk"],
            "wv": inputs["L0_attn_wv"], "wo": inputs["L0_attn_wo"],
            "rel_emb": inputs["L0_attn_rel_emb"],
            "W1": inputs["L0_ffn_W1"],
            "b1": inputs["L0_ffn_b1"].reshape(1, FF),
            "W2": inputs["L0_ffn_W2"],
            "b2": inputs["L0_ffn_b2"].reshape(1, D),
            "wmod": inputs["L0_wmod"],
        }
        m.update(cons)
        in_maps.append(m)
    res = run_bass_kernel_spmd(nc, in_maps, core_ids=list(range(8)))
    return np.stack([r["y"] for r in res.results], axis=0)
